# version 01 (final)
# baseline (speedup 1.0000x reference)
import numpy as np
import ml_dtypes
from contextlib import ExitStack

import concourse.bass as bass
import concourse.mybir as mybir
from concourse.bass_utils import run_bass_kernel_spmd

F32 = mybir.dt.float32
BF16 = mybir.dt.bfloat16
AF = mybir.ActivationFunctionType
ALU = mybir.AluOpType
BF = ml_dtypes.bfloat16

MODE = "unfused"
DEBUG_SPILL = False
DEBUG_W2HAT = False
L1A_SKIP_ROPE = False
L1A_ROPE_LEVEL = 3
L1A_SKIP_MASK = False
L1A_NHEADS = 16
L0A_NO_QROW = False
BARRIER_WO = False

D = 2048
T = 2048
NCORE = 8
EC = D // 128
TC = T // 128
TG = 4
GW = 512
RS128 = 0.08838834764831845
RS256 = 0.0625
SW2 = 2.0 ** (-6.5)

ENGINES = ("tensor", "vector", "scalar", "gpsimd", "sync")



class Opr:
    __slots__ = ("eng", "fn", "deps", "raw", "dma", "tiny", "marked", "val",
                 "tag")

    def __init__(self, eng, fn, dma, tag="", tiny=False):
        self.eng = eng
        self.fn = fn
        self.deps = []
        self.raw = set()
        self.dma = dma
        self.tiny = tiny
        self.marked = False
        self.val = None
        self.tag = tag


class Buf:
    __slots__ = ("name", "writers", "readers")

    def __init__(self, name):
        self.name = name
        self.writers = []
        self.readers = []


class Region:
    def __init__(self, ap, name, single=False):
        self.ap = ap
        self.name = name
        self.bufs = {}
        self.carry = []
        self.single = single

    def buf(self, key=None):
        if self.single:
            key = "_"
        if key not in self.bufs:
            b = Buf(f"{self.name}:{key}")
            b.writers = list(self.carry)
            self.bufs[key] = b
        return self.bufs[key]

    def new_epoch(self):
        carry = []
        seen = set()
        for b in self.bufs.values():
            for o in b.writers + b.readers:
                if id(o) not in seen:
                    seen.add(id(o))
                    carry.append(o)
        for o in self.carry:
            if id(o) not in seen:
                seen.add(id(o))
                carry.append(o)
        self.carry = carry
        self.bufs = {}


class Sched:
    def __init__(self):
        self.ops = {e: [] for e in ENGINES}
        self.all_ops = []
        self.barriers = []

    def barrier(self):
        mark = {e: len(self.ops[e]) for e in ENGINES}
        bid = len(self.barriers)
        self.barriers.append(mark)
        for e in ENGINES:
            self.ops[e].append(("BARRIER", bid))

    def op(self, eng, fn, reads=(), writes=(), dma=False, tag="", tiny=False):
        o = Opr(eng, fn, dma, tag, tiny)
        deps = []
        for b in reads:
            if not b.writers and not b.name.startswith("dram"):
                import sys
                print(f"UBD-WARN: {o.tag} reads never-written {b.name}",
                      file=sys.stderr)
            deps.extend(b.writers)
            for p in b.writers:
                o.raw.add(id(p))
        for b in writes:
            deps.extend(b.writers)
            deps.extend(b.readers)
        o.deps = deps
        for b in reads:
            b.readers.append(o)
        for b in writes:
            b.writers = [o]
            b.readers = []
        self.ops[eng].append(o)
        self.all_ops.append(o)
        return o

    def finalize_and_emit(self, block, sems):
        for o in self.all_ops:
            pruned = []
            seen = set()
            for p in o.deps:
                if id(p) in seen or p is o:
                    continue
                seen.add(id(p))
                if p.eng == o.eng:
                    if p.dma and o.dma:
                        if id(p) in o.raw:
                            pruned.append(p)
                        continue
                    if not p.dma and not o.dma:
                        continue
                    pruned.append(p)
                    continue
                pruned.append(p)
            o.deps = pruned
            for p in pruned:
                p.marked = True
        dc = {e: 0 for e in ENGINES}
        tc_ = {e: 0 for e in ENGINES}
        cc = {e: 0 for e in ENGINES}
        for e in ENGINES:
            for o in self.ops[e]:
                if isinstance(o, tuple):
                    continue
                if o.dma and o.tiny:
                    tc_[e] += 16
                    o.val = tc_[e]
                elif o.dma:
                    dc[e] += 16
                    o.val = dc[e]
                elif o.marked:
                    cc[e] += 1
                    o.val = cc[e]
        gpos = {}
        for gi, o in enumerate(self.all_ops):
            gpos[id(o)] = gi
        qinfo = {}
        for qe in ("sync", "gpsimd", "scalar"):
            dmas = [o for o in self.ops[qe]
                    if not isinstance(o, tuple) and o.dma and not o.tiny]
            qidx = {id(o): i + 1 for i, o in enumerate(dmas)}
            qpos = sorted(gpos[id(o)] for o in dmas)
            qinfo[qe] = (qidx, qpos, len(dmas))
        import bisect
        fence_sets = {qe: set(range(4, qinfo[qe][2] + 1, 4)) |
                      ({qinfo[qe][2]} if qinfo[qe][2] else set())
                      for qe in qinfo}
        for o in self.all_ops:
            if isinstance(o, tuple):
                continue
            for p in o.deps:
                if not (p.dma and not p.tiny) or p.eng == o.eng:
                    continue
                qidx, qpos, qn = qinfo[p.eng]
                j = qidx[id(p)]
                last_before = bisect.bisect_left(qpos, gpos[id(o)])
                fs = fence_sets[p.eng]
                f = None
                for cand in range(j, last_before + 1):
                    if cand in fs:
                        f = cand
                        break
                if f is None:
                    fs.add(j)

        def covering_fence(qe, j, consumer_gpos):
            qidx, qpos, qn = qinfo[qe]
            last_before = bisect.bisect_left(qpos, consumer_gpos)
            fs = fence_sets[qe]
            for cand in range(j, last_before + 1):
                if cand in fs:
                    return cand
            return j
        def emit_engine(eng_obj, ename):
            waited = {}
            for o in self.ops[ename]:
                if isinstance(o, tuple):
                    bid = o[1]
                    for e in ENGINES:
                        nd, nm = bar_vals[bid][e]
                        if e == ename:
                            continue
                        if nd:
                            sem = sems[("dma", e)]
                            if nd > waited.get(id(sem), 0):
                                eng_obj.wait_ge(sem, nd)
                                waited[id(sem)] = nd
                        if nm:
                            sem = sems[("cmp", e)]
                            if nm > waited.get(id(sem), 0):
                                eng_obj.wait_ge(sem, nm)
                                waited[id(sem)] = nm
                    continue
                wants = {}
                for p in o.deps:
                    kind = "dmat" if (p.dma and p.tiny) else (
                        "dma" if p.dma else "cmp")
                    if p.val is None:
                        raise RuntimeError(f"unvalued dep {p.tag} <- {o.tag}")
                    v = p.val
                    if kind == "dma" and p.eng != o.eng:
                        qidx, _qp, _qn = qinfo[p.eng]
                        v = 16 * covering_fence(p.eng, qidx[id(p)],
                                                gpos[id(o)])
                    key = (p.eng, kind)
                    wants[key] = max(wants.get(key, 0), v)
                for (peng, kind) in sorted(wants):
                    v = wants[(peng, kind)]
                    sem = sems[(kind, peng)]
                    if v > waited.get(id(sem), 0):
                        eng_obj.wait_ge(sem, v)
                        waited[id(sem)] = v
                ins = o.fn(eng_obj)
                if o.dma and o.tiny:
                    ins.then_inc(sems[("dmat", ename)], 16)
                    sem = sems[("dmat", ename)]
                    eng_obj.wait_ge(sem, o.val)
                    waited[id(sem)] = o.val
                elif o.dma:
                    ins.then_inc(sems[("dma", ename)], 16)
                    if (o.val // 16) in fence_sets.get(ename, ()):
                        sem = sems[("dma", ename)]
                        eng_obj.wait_ge(sem, o.val)
                        waited[id(sem)] = o.val
                elif o.marked:
                    if ins is None:
                        raise RuntimeError(f"marked op w/o instruction: {o.tag}")
                    ins.then_inc(sems[("cmp", ename)], 1)
            if ename == "sync":
                for qe in ("sync", "gpsimd"):
                    if dc[qe]:
                        eng_obj.wait_ge(sems[("dma", qe)], dc[qe])
                    if tc_[qe]:
                        eng_obj.wait_ge(sems[("dmat", qe)], tc_[qe])

        block.tensor(lambda t: emit_engine(t, "tensor"))
        block.vector(lambda v: emit_engine(v, "vector"))
        block.scalar(lambda s: emit_engine(s, "scalar"))
        block.gpsimd(lambda g: emit_engine(g, "gpsimd"))
        block.sync(lambda s: emit_engine(s, "sync"))



def _cc(a):
    return np.ascontiguousarray(a)


def _host_constants():
    c = {}
    tri = np.triu(np.ones((128, 128), np.float32))
    c["c_triu_bf"] = tri.astype(BF)
    c["c_ones_f32"] = np.ones((128, 128), np.float32)
    c["c_ones_bf"] = np.ones((128, 128), BF)
    c["c_ident_bf"] = np.eye(128, dtype=np.float32).astype(BF)
    c["c_trils_f32"] = _cc(1.0 - tri)
    c["c_triu_f32"] = tri
    mc = np.zeros((4, 128, 512), np.float32)
    for p in range(4):
        for j in range(4):
            blk = mc[p][:, j * 128:(j + 1) * 128]
            blk[:] = 0.0 if j < p else (tri if j == p else 1.0)
    c["c_mask_causal"] = mc.astype(BF)
    ml = np.zeros((8, 128, 512), np.float32)
    for r in range(8):
        for j in range(4):
            rel = j + 4 - r
            blk = ml[r][:, j * 128:(j + 1) * 128]
            if rel < 0 or rel > 4:
                blk[:] = 0.0
            elif rel == 0:
                blk[:] = tri
            elif rel == 4:
                blk[:] = 1.0 - tri
            else:
                blk[:] = 1.0
    c["c_mask_l1a"] = ml.astype(BF)
    nm = np.zeros((4, 128, 512), np.float32)
    for p in range(4):
        for j in range(4):
            blk = nm[p][:, j * 128:(j + 1) * 128]
            blk[:] = (-30000.0 if j < p else 0.0)
            if j == p:
                blk[:] = np.where(tri > 0, 0.0, -30000.0)
    c["c_nmask_causal"] = nm.astype(BF)
    import math
    H = 16
    start = 2 ** (-2 ** (-(math.log2(H) - 3)))
    slopes = np.asarray([start * start ** i for i in range(H)], np.float32)
    ii = np.arange(128, dtype=np.float32)[:, None]
    cols = np.zeros((128, H * 16), np.float32)
    for h in range(H):
        for sc in range(16):
            cols[:, h * 16 + sc][:, None] = slopes[h] * (128 * sc + ii)
    c["c_alibi_cols"] = cols
    qrow = np.zeros((64, 512), np.float32)
    for h in range(H):
        for g in range(4):
            qrow[h * 4 + g] = slopes[h] * (512 * g + np.arange(512))
    c["c_alibi_qrow"] = qrow
    c["c_nrsinv"] = np.full((128, 128), -np.sqrt(128.0), np.float32)
    c["c_ident_f32"] = np.eye(128, dtype=np.float32)
    t = np.arange(T, dtype=np.float64)
    inv0 = 1.0 / (10000.0 ** (np.arange(0, 256, 2, dtype=np.float64) / 256.0))
    f0 = np.outer(inv0, t)
    c["c_cos0b"] = np.cos(f0).astype(BF)
    c["c_sin0b"] = np.sin(f0).astype(BF)
    inv1 = 1.0 / (10000.0 ** (np.arange(0, 128, 2, dtype=np.float64) / 128.0))
    f1 = np.outer(inv1, t)
    cos1 = np.concatenate([np.cos(f1), np.cos(f1)], axis=0)
    sin1 = np.concatenate([np.sin(f1), np.sin(f1)], axis=0)
    sgn = np.ones((128, 1))
    sgn[:64] = -1.0
    c["c_cos1"] = cos1.astype(BF)
    c["c_sin1s"] = (sin1 * sgn).astype(BF)
    h1 = np.array([[1.0]])
    while h1.shape[0] < 128:
        h1 = np.block([[h1, h1], [h1, -h1]])
    c["c_h128"] = h1.astype(BF)
    trig = np.array([[sx, sy, sz] for sx in (1., -1.) for sy in (1., -1.)
                     for sz in (1., -1.)], np.float32) / np.sqrt(3.0)
    c["c_dirs"] = _cc(trig[:8].T).astype(BF)
    return c


def _prep_weights(inp):
    w = {}
    w["w_l0a_qT"] = _cc(np.asarray(inp["L0_attn_a_Wq"]).T)
    w["w_l0a_kT"] = _cc(np.asarray(inp["L0_attn_a_Wk"]).T)
    w["w_l0a_vT"] = _cc(np.asarray(inp["L0_attn_a_Wv"]).T)
    w["w_l0a_oT"] = _cc(np.asarray(inp["L0_attn_a_Wo"]).T)
    for nm, key in (("bq", "L0_attn_a_bq"), ("bk", "L0_attn_a_bk"),
                    ("bv", "L0_attn_a_bv"), ("bo", "L0_attn_a_bo")):
        w[f"w_l0a_{nm}"] = _cc(np.asarray(inp[key]).reshape(16, 128).T)
    w["w_l0b_q"] = _cc(np.asarray(inp["L0_attn_b_Wq"]))
    w["w_l0b_k"] = _cc(np.asarray(inp["L0_attn_b_Wk"]))
    w["w_l0b_v"] = _cc(np.asarray(inp["L0_attn_b_Wv"]))
    w["w_l0b_o"] = _cc(np.asarray(inp["L0_attn_b_Wo"]))
    w["w_l0b_hs"] = _cc(np.asarray(inp["L0_attn_b_head_scales"]).reshape(1, 8))
    w["w_f0_gT"] = _cc(np.asarray(inp["L0_ffn_gate_w"]).T)
    w["w_f0_uT"] = _cc(np.asarray(inp["L0_ffn_up_w"]).T)
    w["w_f0_dT"] = _cc(np.asarray(inp["L0_ffn_down_w"]).T)
    w["w_f0_gs"] = _cc(np.asarray(inp["L0_ffn_gate_s"]).reshape(44, 128).T)
    w["w_f0_us"] = _cc(np.asarray(inp["L0_ffn_up_s"]).reshape(44, 128).T)
    w["w_f0_ds"] = _cc(np.asarray(inp["L0_ffn_down_s"]).reshape(16, 128).T)
    perm = np.zeros(D, np.int64)
    for h in range(16):
        b = h * 128
        perm[b:b + 64] = b + np.arange(0, 128, 2)
        perm[b + 64:b + 128] = b + np.arange(1, 128, 2)
    w["w_l1a_qT"] = _cc(np.asarray(inp["L1_attn_a_Wq"]).T[:, perm])
    w["w_l1a_kT"] = _cc(np.asarray(inp["L1_attn_a_Wk"]).T[:, perm])
    w["w_l1a_vT"] = _cc(np.asarray(inp["L1_attn_a_Wv"]).T)
    w["w_l1a_oT"] = _cc(np.asarray(inp["L1_attn_a_Wo"]).T)
    w["w_l1b_qT"] = _cc(np.asarray(inp["L1_attn_b_Wq"]).T)
    w["w_l1b_kT"] = _cc(np.asarray(inp["L1_attn_b_Wk"]).T)
    w["w_l1b_vT"] = _cc(np.asarray(inp["L1_attn_b_Wv"]).T)
    w["w_l1b_oT"] = _cc(np.asarray(inp["L1_attn_b_Wo"]).T)
    w["w_l1b_sqT"] = _cc(np.asarray(inp["L1_attn_b_Wsq"]).T)
    w["w_l1b_skT"] = _cc(np.asarray(inp["L1_attn_b_Wsk"]).T)
    w["w_l1b_svT"] = _cc(np.asarray(inp["L1_attn_b_Wsv"]).T)
    w["w_l1b_fT"] = _cc(np.asarray(inp["L1_attn_b_Wf"]).T)
    w["w_l1b_bf"] = _cc(np.asarray(inp["L1_attn_b_bf"]).reshape(1, 16))
    w["w_f1_1T"] = _cc(np.asarray(inp["L1_ffn_W1"]).T)
    w["w_f1_3T"] = _cc(np.asarray(inp["L1_ffn_W3"]).T)
    w["w_f1_2T"] = _cc(np.asarray(inp["L1_ffn_W2"]).T)
    return w


SUBLAYERS = ["l0a", "ffn0a", "l0b", "ffn0b", "l1a", "ffn1a", "l1b", "ffn1b"]

SUB_W = {
    "l0a": ["w_l0a_qT", "w_l0a_kT", "w_l0a_vT", "w_l0a_oT",
            "w_l0a_bq", "w_l0a_bk", "w_l0a_bv", "w_l0a_bo",
            "c_alibi_cols", "c_alibi_qrow", "c_mask_causal", "c_nrsinv"],
    "ffn0": ["w_f0_gT", "w_f0_uT", "w_f0_dT", "w_f0_gs", "w_f0_us", "w_f0_ds"],
    "l0b": ["w_l0b_q", "w_l0b_k", "w_l0b_v", "w_l0b_o", "w_l0b_hs",
            "c_cos0b", "c_sin0b", "c_dirs", "c_mask_causal"],
    "l1a": ["w_l1a_qT", "w_l1a_kT", "w_l1a_vT", "w_l1a_oT",
            "c_cos1", "c_sin1s", "c_mask_l1a"],
    "ffn1": ["w_f1_1T", "w_f1_3T", "w_f1_2T", "c_h128"],
    "l1b": ["w_l1b_qT", "w_l1b_kT", "w_l1b_vT", "w_l1b_oT",
            "w_l1b_sqT", "w_l1b_skT", "w_l1b_svT", "w_l1b_fT", "w_l1b_bf",
            "c_cos1", "c_sin1s", "c_mask_causal", "c_triu_f32",
            "c_ident_bf", "c_ident_f32", "c_nrsinv"],
}
BASE_W = ["c_ones_f32", "c_ones_bf", "c_triu_bf"]


def sub_weight_names(sub):
    key = {"ffn0a": "ffn0", "ffn0b": "ffn0", "ffn1a": "ffn1",
           "ffn1b": "ffn1"}.get(sub, sub)
    return SUB_W[key] + BASE_W


class Ctx:
    pass



def dram_load(ctx, eng, dst_ap, dst_buf, src_ap, tag):
    ctx.S.op(eng, lambda e, o=dst_ap, i=src_ap: e.dma_start(out=o, in_=i),
             writes=[dst_buf], dma=True, tag=tag, tiny=True)


def slab_load(ctx, dst_ap, dst_buf, dram, rows, col0, cols, tag):
    n = rows // 128
    src = dram[0:rows, col0:col0 + cols].rearrange("(c p) n -> p c n", p=128)
    dst = dst_ap[:, 0:n * cols].rearrange("p (c n) -> p c n", c=n)
    ctx.S.op("gpsimd", lambda e, o=dst, i=src: e.dma_start(out=o, in_=i),
             writes=[dst_buf], dma=True, tag=tag)


def load_consts(ctx):
    if "ones_f" not in ctx.MISC.bufs:
        mf, mb = ctx.MISC.ap, ctx.MISB.ap
        dram_load(ctx, "sync", mf[:, 0:128], ctx.MISC.buf("ones_f"),
                  ctx.dram["c_ones_f32"], "ld_onesf")
        dram_load(ctx, "sync", mb[:, 0:128], ctx.MISB.buf("ones_b"),
                  ctx.dram["c_ones_bf"], "ld_onesb")
        dram_load(ctx, "sync", mb[:, 128:256], ctx.MISB.buf("triu_b"),
                  ctx.dram["c_triu_bf"], "ld_triub")
        ctx.S.op("vector", lambda e: e.memset(mb[:, 512:1024], 0.0),
                 writes=[ctx.MISB.buf("zeros")], tag="ld_zeros")


def onesf(ctx):
    return ctx.MISC.ap[:, 0:128], ctx.MISC.buf("ones_f")


def onesb(ctx):
    return ctx.MISB.ap[:, 0:128], ctx.MISB.buf("ones_b")


def lnxt_ap(ctx, ic, t0, wid):
    return ctx.LNXT.ap[:, ic * T + t0: ic * T + t0 + wid]



def emit_ln(ctx, src_ap, src_buf, tag):
    S = ctx.S
    for r in (ctx.XQ, ctx.XR, ctx.PRB, ctx.TMPF, ctx.SMF):
        r.new_epoch()
    load_consts(ctx)
    of_ap, of_b = onesf(ctx)
    ob_ap, ob_b = onesb(ctx)
    for r in (ctx.RZ,):
        r.new_epoch()
    smf = ctx.SMF.ap
    mrow = smf[0:1, 0:512]
    vrow = ctx.RZ.ap[0:1, 0:512]
    for tg in range(TG):
        t0 = tg * GW
        psum = ctx.PS[6]
        psq = ctx.PS[5]
        pb_sum = psum.buf()
        pb_sq = psq.buf()
        mean_acc_b = ctx.SMF.buf(("mean", tg))
        var_acc_b = ctx.RZ.buf(("lnvq", tg))
        xb_bufs = []
        for s0 in range(0, EC, 4):
            seg_tiles = []
            for ecl in range(4):
                ec = s0 + ecl
                xr_i = ec % 3
                xr_buf = ctx.XR.buf(("lnx", xr_i))
                xr_ap = ctx.XR.ap[:, xr_i * GW:(xr_i + 1) * GW]
                S.op("sync", lambda e, o=xr_ap,
                     i=src_ap[ec * 128:(ec + 1) * 128, t0:t0 + GW]:
                     e.dma_start(out=o, in_=i),
                     reads=[src_buf], writes=[xr_buf], dma=True, tag=f"{tag}ln_ldx")
                xb_buf = ctx.XQ.buf(("xb", ec))
                xb_bufs.append(xb_buf)
                xb_ap = ctx.XQ.ap[:, ec * GW:(ec + 1) * GW]
                S.op("scalar", lambda e, o=xb_ap, i=xr_ap: e.copy(o, i),
                     reads=[xr_buf], writes=[xb_buf], tag=f"{tag}ln_cast")
                sq_i = ec % 4
                sq_buf = ctx.PRB.buf(("sq", sq_i))
                sq_ap = ctx.PRB.ap[:, sq_i * GW:(sq_i + 1) * GW]
                S.op("gpsimd", lambda e, o=sq_ap, i=xr_ap: e.tensor_mul(o, i, i),
                     reads=[xr_buf], writes=[sq_buf], tag=f"{tag}ln_sq")
                seg_tiles.append((ec, xb_ap, xb_buf, sq_ap, sq_buf))
            for ec, xb_ap, xb_buf, sq_ap, sq_buf in seg_tiles:
                S.op("tensor", lambda e, o=psum.ap[0:1, :], l=ob_ap[:, 0:1],
                     r=xb_ap, s=(ec % 4 == 0), p=(ec % 4 == 3):
                     e.matmul(o, l, r, start=s, stop=p),
                     reads=[xb_buf, ob_b], writes=[pb_sum], tag=f"{tag}ln_mm1")
            for ec, xb_ap, xb_buf, sq_ap, sq_buf in seg_tiles:
                S.op("tensor", lambda e, o=psq.ap[0:1, :], l=ob_ap[:, 0:1],
                     r=sq_ap, s=(ec % 4 == 0), p=(ec % 4 == 3):
                     e.matmul(o, l, r, start=s, stop=p),
                     reads=[sq_buf, ob_b], writes=[pb_sq], tag=f"{tag}ln_mm2")
            if s0 == 0:
                S.op("scalar", lambda e: e.copy(mrow, psum.ap[0:1, :]),
                     reads=[pb_sum], writes=[mean_acc_b], tag=f"{tag}ln_a1")
                S.op("scalar", lambda e: e.copy(vrow, psq.ap[0:1, :]),
                     reads=[pb_sq], writes=[var_acc_b], tag=f"{tag}ln_a2")
            else:
                S.op("vector", lambda e: e.tensor_add(mrow, psum.ap[0:1, :],
                                                      mrow),
                     reads=[pb_sum, mean_acc_b], writes=[mean_acc_b],
                     tag=f"{tag}ln_a1")
                S.op("vector", lambda e: e.tensor_add(vrow, psq.ap[0:1, :],
                                                      vrow),
                     reads=[pb_sq, var_acc_b], writes=[var_acc_b],
                     tag=f"{tag}ln_a2")
        mean_b = mean_acc_b
        var_b = var_acc_b
        w_b = ctx.TMPF.buf(("lnw", tg))
        r_b = ctx.TMPF.buf(("lnr", tg))
        wrow = ctx.TMPF.ap[0:1, 2 * GW:2 * GW + 512]
        rrow = ctx.TMPF.ap[0:1, 3 * GW:3 * GW + 512]
        S.op("scalar", lambda e: e.mul(mrow, mrow, 1.0 / D),
             reads=[mean_b], writes=[mean_b], tag=f"{tag}ln_mean")
        S.op("scalar", lambda e: e.mul(vrow, vrow, 1.0 / D),
             reads=[var_b], writes=[var_b], tag=f"{tag}ln_ex2")
        S.op("vector", lambda e: e.tensor_mul(wrow, mrow, mrow),
             reads=[mean_b], writes=[w_b], tag=f"{tag}ln_m2")
        S.op("vector", lambda e: e.tensor_sub(vrow, vrow, wrow),
             reads=[w_b, var_b], writes=[var_b], tag=f"{tag}ln_var")
        S.op("vector", lambda e: e.tensor_scalar_add(vrow, vrow, 1e-5),
             reads=[var_b], writes=[var_b], tag=f"{tag}ln_eps")
        S.op("scalar", lambda e: e.activation(wrow, vrow, AF.Sqrt),
             reads=[var_b], writes=[w_b], tag=f"{tag}ln_std")
        S.op("vector", lambda e: e.reciprocal(rrow, wrow),
             reads=[w_b], writes=[r_b], tag=f"{tag}ln_r0")
        S.op("vector", lambda e: e.tensor_mul(wrow, rrow, rrow),
             reads=[r_b], writes=[w_b], tag=f"{tag}ln_n1r")
        S.op("vector", lambda e: e.tensor_mul(wrow, wrow, vrow),
             reads=[w_b, var_b], writes=[w_b], tag=f"{tag}ln_n2r")
        S.op("vector", lambda e: e.tensor_scalar(wrow, wrow, -0.5, 1.5,
                                                 ALU.mult, ALU.add),
             reads=[w_b], writes=[w_b], tag=f"{tag}ln_n3r")
        S.op("vector", lambda e: e.tensor_mul(rrow, rrow, wrow),
             reads=[w_b, r_b], writes=[r_b], tag=f"{tag}ln_n4r")
        S.op("vector", lambda e: e.tensor_mul(wrow, mrow, rrow),
             reads=[mean_b, r_b], writes=[w_b], tag=f"{tag}ln_mr")
        pb1, pb2 = ctx.PS[5], ctx.PS[6]
        b1, b2 = pb1.buf(), pb2.buf()
        S.op("tensor", lambda e: e.matmul(pb1.ap[:], of_ap[0:1, :], rrow,
                                          start=True, stop=True),
             reads=[r_b, of_b], writes=[b1], tag=f"{tag}ln_bc1")
        S.op("tensor", lambda e: e.matmul(pb2.ap[:], of_ap[0:1, :], wrow,
                                          start=True, stop=True),
             reads=[w_b, of_b], writes=[b2], tag=f"{tag}ln_bc2")
        rstdb_buf = ctx.XR.buf("rstdb")
        mrb_buf = ctx.XR.buf("mrb")
        rstdb_ap = ctx.XR.ap[:, 3 * GW:4 * GW]
        mrb_ap = ctx.XR.ap[:, 4 * GW:5 * GW]
        S.op("vector", lambda e: e.tensor_copy(rstdb_ap, pb1.ap[:]),
             reads=[b1], writes=[rstdb_buf], tag=f"{tag}ln_cb1")
        S.op("scalar", lambda e: e.copy(mrb_ap, pb2.ap[:]),
             reads=[b2], writes=[mrb_buf], tag=f"{tag}ln_cb2")
        for ec in range(EC):
            xb_buf = xb_bufs[ec]
            xb_ap = ctx.XQ.ap[:, ec * GW:(ec + 1) * GW]
            tmp_i = ec % 2
            tmp_buf = ctx.TMPF.buf(("lnt", tmp_i))
            tmp_ap = ctx.TMPF.ap[:, tmp_i * GW:(tmp_i + 1) * GW]
            ln_buf = ctx.LNXT.buf((ec, tg))
            ln_ap = lnxt_ap(ctx, ec, t0, GW)
            eng = "vector" if ec % 2 == 0 else "gpsimd"
            S.op(eng, lambda e, o=tmp_ap, a=xb_ap, b=rstdb_ap: e.tensor_mul(o, a, b),
                 reads=[xb_buf, rstdb_buf], writes=[tmp_buf], tag=f"{tag}ln_n1")
            S.op(eng, lambda e, o=ln_ap, a=tmp_ap, b=mrb_ap: e.tensor_sub(o, a, b),
                 reads=[tmp_buf, mrb_buf], writes=[ln_buf], tag=f"{tag}ln_n2")



def attn_core(ctx, tag, h, g, sc_list, gen_scores, exp_scale, bias_fn,
              mask_fn, v_fn, n_dc, pv_base, z_row, premask_fn=None):
    S = ctx.S
    ob_ap, ob_b = onesb(ctx)
    n = len(sc_list)
    nseg = (n + 7) // 8
    assert nseg <= 2
    z_bank = ctx.PS[6]
    z_buf = z_bank.buf()
    pv_banks = [[ctx.PS[pv_base + dc * nseg + m] for m in range(nseg)]
                for dc in range(n_dc)]
    pv_bufs = [[b.buf() for b in row] for row in pv_banks]
    for m in range(nseg):
        seg = sc_list[m * 8:(m + 1) * 8]
        pr_tiles = []
        for k, sc in enumerate(seg):
            sbk = ctx.PS[k % 2]
            sb_buf = sbk.buf()
            gen_scores(sc, sbk, sb_buf)
            pr_i = k
            pr_buf = ctx.PRB.buf(("pr", pr_i))
            pr_ap = ctx.PRB.ap[:, pr_i * GW:(pr_i + 1) * GW]
            bias = bias_fn(sc) if bias_fn else None
            if bias is not None:
                S.op("scalar", lambda e, o=pr_ap, i=sbk.ap[:], b=bias[0]:
                     e.activation(o, i, AF.Exp, bias=b, scale=exp_scale),
                     reads=[sb_buf, bias[1]], writes=[pr_buf],
                     tag=f"{tag}exp{h}_{g}_{sc}")
            else:
                S.op("scalar", lambda e, o=pr_ap, i=sbk.ap[:]:
                     e.activation(o, i, AF.Exp, scale=exp_scale),
                     reads=[sb_buf], writes=[pr_buf],
                     tag=f"{tag}exp{h}_{g}_{sc}")
            mask = mask_fn(sc) if mask_fn else None
            if mask is not None:
                S.op("vector", lambda e, o=pr_ap, a=pr_ap, b=mask[0]:
                     e.tensor_mul(o, a, b),
                     reads=[pr_buf, mask[1]], writes=[pr_buf],
                     tag=f"{tag}msk{h}_{g}_{sc}")
            pre = premask_fn(sc) if premask_fn else None
            if pre is not None:
                S.op("vector", lambda e, o=pr_ap, a=pr_ap:
                     e.tensor_scalar_min(o, a, 1e30),
                     reads=[pr_buf], writes=[pr_buf],
                     tag=f"{tag}clp{h}_{g}_{sc}")
                S.op("vector", lambda e, o=pr_ap, a=pr_ap, b=pre[0]:
                     e.tensor_mul(o, a, b),
                     reads=[pr_buf, pre[1]], writes=[pr_buf],
                     tag=f"{tag}msk2{h}_{g}_{sc}")
            pr_tiles.append((sc, pr_ap, pr_buf))
        zr = z_row + 32 * m
        for k, (sc, pr_ap, pr_buf) in enumerate(pr_tiles):
            S.op("tensor", lambda e, o=z_bank.ap[zr:zr + 1, :],
                 l=ob_ap[:, 0:1], r=pr_ap, s=(k == 0),
                 p=(k == len(pr_tiles) - 1):
                 e.matmul(o, l, r, start=s, stop=p),
                 reads=[pr_buf, ob_b], writes=[z_buf],
                 tag=f"{tag}z{h}_{g}_{sc}")
        for dc in range(n_dc):
            for k, (sc, pr_ap, pr_buf) in enumerate(pr_tiles):
                vap, vbuf = v_fn(sc, dc)
                S.op("tensor", lambda e, o=pv_banks[dc][m].ap[:], l=vap,
                     r=pr_ap, s=(k == 0), p=(k == len(pr_tiles) - 1):
                     e.matmul(o, l, r, start=s, stop=p),
                     reads=[pr_buf, vbuf], writes=[pv_bufs[dc][m]],
                     tag=f"{tag}pv{h}_{g}_{sc}_{dc}")
    return pv_banks, pv_bufs, z_buf, nseg


def pv_merge(ctx, tag, h, g, pv_banks, pv_bufs, nseg, dc, slot):
    S = ctx.S
    t_buf = ctx.TMPF.buf(("nrm", slot))
    t_ap = ctx.TMPF.ap[:, slot * GW:(slot + 1) * GW]
    if nseg == 1:
        S.op("scalar", lambda e, o=t_ap, i=pv_banks[dc][0].ap[:]: e.copy(o, i),
             reads=[pv_bufs[dc][0]], writes=[t_buf], tag=f"{tag}pm{h}_{g}_{dc}")
    else:
        S.op("scalar", lambda e, o=t_ap, i=pv_banks[dc][0].ap[:]: e.copy(o, i),
             reads=[pv_bufs[dc][0]], writes=[t_buf], tag=f"{tag}pm{h}_{g}_{dc}")
        S.op("vector", lambda e, o=t_ap, a=pv_banks[dc][1].ap[:], b=t_ap:
             e.tensor_add(o, a, b),
             reads=[pv_bufs[dc][1], t_buf], writes=[t_buf],
             tag=f"{tag}pa{h}_{g}_{dc}")
    return t_ap, t_buf


def softmax_norm(ctx, tag, h, g, z_buf, z_row, slot, nseg=1):
    S = ctx.S
    of_ap, of_b = onesf(ctx)
    rz_ap = ctx.RZ.ap[z_row:z_row + 1, 0:512]
    rz_buf = ctx.RZ.buf(("rz", z_row))
    if nseg == 1:
        S.op("vector", lambda e, o=rz_ap, i=ctx.PS[6].ap[z_row:z_row + 1, :]:
             e.reciprocal(o, i),
             reads=[z_buf], writes=[rz_buf], tag=f"{tag}rz{h}_{g}")
    else:
        S.op("scalar", lambda e, o=rz_ap, i=ctx.PS[6].ap[z_row:z_row + 1, :]:
             e.copy(o, i),
             reads=[z_buf], writes=[rz_buf], tag=f"{tag}rzc{h}_{g}")
        S.op("vector", lambda e, o=rz_ap,
             a=ctx.PS[6].ap[z_row + 32:z_row + 33, :], b=rz_ap:
             e.tensor_add(o, a, b),
             reads=[z_buf, rz_buf], writes=[rz_buf], tag=f"{tag}rza{h}_{g}")
        S.op("vector", lambda e, o=rz_ap, i=rz_ap: e.reciprocal(o, i),
             reads=[rz_buf], writes=[rz_buf], tag=f"{tag}rz{h}_{g}")
    bk = ctx.PS[slot % 2]
    bk_buf = bk.buf()
    S.op("tensor", lambda e, o=bk.ap[:], l=of_ap[z_row:z_row + 1, :], r=rz_ap:
         e.matmul(o, l, r, start=True, stop=True),
         reads=[rz_buf, of_b], writes=[bk_buf], tag=f"{tag}rzbc{h}_{g}")
    rzb_buf = ctx.TMPF.buf(("rzb", slot))
    rzb_ap = ctx.TMPF.ap[:, (2 + slot) * GW:(3 + slot) * GW]
    S.op("scalar", lambda e, o=rzb_ap, i=bk.ap[:]: e.copy(o, i),
         reads=[bk_buf], writes=[rzb_buf], tag=f"{tag}rzcp{h}_{g}")
    return rzb_ap, rzb_buf


def spill_store(ctx, tag, row0, g, o_ap, o_buf):
    ctx.S.op("sync", lambda e,
             o=ctx.dram["spill"][row0:row0 + 128, g * GW:(g + 1) * GW],
             i=o_ap: e.dma_start(out=o, in_=i),
             reads=[o_buf], writes=[ctx.spill_buf], dma=True,
             tag=f"{tag}spst{row0}_{g}")



def emit_wo_residual(ctx, src_ap, src_buf, dst_ap, dst_buf, wo_dram, bo, tag):
    S = ctx.S
    if BARRIER_WO:
        S.barrier()
    ctx.WA.new_epoch()
    ctx.XQ.new_epoch()
    ctx.XR.new_epoch()
    spill = ctx.dram["spill"]
    for og in range(4):
        wbuf = ctx.WA.buf("woslab")
        slab_load(ctx, ctx.WA.ap, wbuf, wo_dram, D, og * 512, 512,
                  f"{tag}wo_lds{og}")
        wap = ctx.WA.ap
        for tg in range(TG):
            t0 = tg * GW
            abuf = ctx.XQ.buf("aout")
            aap = ctx.XQ.ap
            S.op("sync", lambda e, o=aap.rearrange("p (c n) -> p c n", c=EC),
                 i=spill[0:D, t0:t0 + GW].rearrange("(c p) n -> p c n", p=128):
                 e.dma_start(out=o, in_=i),
                 reads=[ctx.spill_buf], writes=[abuf], dma=True,
                 tag=f"{tag}wo_lda{og}_{tg}")
            banks = [ctx.PS[j] for j in range(4)]
            bbufs = [b.buf() for b in banks]
            for seg in range(4):
                for j in range(4):
                    for ecl in range(4):
                        ec = seg * 4 + ecl
                        S.op("tensor", lambda e, o=banks[j].ap[:],
                             l=wap[:, ec * 512 + j * 128:ec * 512 + j * 128 + 128],
                             r=aap[:, ec * GW:(ec + 1) * GW],
                             s=(ec == 0), p=(ec == EC - 1):
                             e.matmul(o, l, r, start=s, stop=p),
                             reads=[abuf, wbuf], writes=[bbufs[j]],
                             tag=f"{tag}wo_mm{og}_{tg}_{ec}_{j}")
            for j in range(4):
                oc = og * 4 + j
                x_buf = ctx.XR.buf(("xres", j % 2))
                x_ap = ctx.XR.ap[:, (j % 2) * GW:(j % 2) * GW + GW]
                S.op("sync", lambda e, o=x_ap,
                     i=src_ap[oc * 128:(oc + 1) * 128, t0:t0 + GW]:
                     e.dma_start(out=o, in_=i),
                     reads=[src_buf], writes=[x_buf], dma=True,
                     tag=f"{tag}wo_ldx{og}_{tg}_{j}")
                o_buf = ctx.XR.buf(("xo", 2 + j % 2))
                o_ap = ctx.XR.ap[:, (2 + j % 2) * GW:(2 + j % 2) * GW + GW]
                if bo is not None:
                    S.op("vector", lambda e, o=o_ap, a=banks[j].ap[:],
                         c=bo[0][:, oc:oc + 1], b=x_ap:
                         e.scalar_tensor_tensor(o, a, c, b, ALU.add, ALU.add),
                         reads=[bbufs[j], x_buf, bo[1]], writes=[o_buf],
                         tag=f"{tag}wo_add{og}_{tg}_{j}")
                else:
                    S.op("vector", lambda e, o=o_ap, a=banks[j].ap[:], b=x_ap:
                         e.tensor_add(o, a, b),
                         reads=[bbufs[j], x_buf], writes=[o_buf],
                         tag=f"{tag}wo_add{og}_{tg}_{j}")
                S.op("sync", lambda e,
                     o=dst_ap[oc * 128:(oc + 1) * 128, t0:t0 + GW], i=o_ap:
                     e.dma_start(out=o, in_=i),
                     reads=[o_buf], writes=[dst_buf], dma=True,
                     tag=f"{tag}wo_st{og}_{tg}_{j}")



def emit_l0a(ctx, src_ap, src_buf, dst_ap, dst_buf):
    S = ctx.S
    tag = "a0"
    emit_ln(ctx, src_ap, src_buf, tag)
    for r in (ctx.XQ, ctx.XR, ctx.QK, ctx.WC, ctx.TAB, ctx.PRB):
        r.new_epoch()
    mf = ctx.MISC.ap
    bias = {}
    for bi, nm in enumerate(("bq", "bk", "bv", "bo")):
        b = ctx.MISC.buf(("l0a", nm))
        ap = mf[:, 256 + bi * 16:272 + bi * 16]
        dram_load(ctx, "sync", ap, b, ctx.dram[f"w_l0a_{nm}"], f"{tag}ldb{nm}")
        bias[nm] = (ap, b)
    ali_b = ctx.XR.buf("alibi")
    ali = ctx.XR.ap[:, 3 * GW:3 * GW + 256]
    dram_load(ctx, "sync", ali, ali_b, ctx.dram["c_alibi_cols"], f"{tag}ldali")
    nrs_b = ctx.MISC.buf("nrsinv")
    nrs = mf[64:65, 648:776]
    dram_load(ctx, "sync", mf[:, 648:776], nrs_b, ctx.dram["c_nrsinv"],
              f"{tag}ldnrs")
    msk_b = ctx.TAB.buf("mask")
    msk = ctx.TAB.ap
    S.op("sync", lambda e: e.dma_start(
        out=msk[:, 0:4 * GW].rearrange("p (m n) -> p m n", m=4),
        in_=ctx.dram["c_mask_causal"].rearrange("m p n -> p m n")),
        writes=[msk_b], dma=True, tag=f"{tag}ldmask")

    wq, wk, wv = ctx.dram["w_l0a_qT"], ctx.dram["w_l0a_kT"], ctx.dram["w_l0a_vT"]
    QT = ctx.QK.ap[:, 0:2048]
    KT = ctx.QK.ap[:, 2048:4096]
    V = ctx.QK.ap[:, 4096:6144]

    for h in range(16):
        slabs = {}
        for si, (nm, wt) in enumerate((("q", wq), ("k", wk), ("v", wv))):
            sl = (h * 3 + si) % 4
            wbuf = ctx.WC.buf(("slab", sl))
            wap = ctx.WC.ap[:, sl * 2048:(sl + 1) * 2048]
            slab_load(ctx, wap, wbuf, wt, D, h * 128, 128, f"{tag}lds{h}{nm}")
            slabs[nm] = (wap, wbuf)
        qt_buf = ctx.QK.buf(("qt", h))
        kt_buf = ctx.QK.buf(("kt", h))
        v_buf = ctx.QK.buf(("v", h))
        for nm, dstt, dbuf, bcol in (("q", QT, qt_buf, bias["bq"]),
                                     ("k", KT, kt_buf, bias["bk"])):
            wap, wbuf = slabs[nm]
            for tg in range(TG):
                pb = ctx.PS[5 if tg % 2 == 0 else 6]
                pbuf = pb.buf()
                for ic in range(EC):
                    S.op("tensor", lambda e, o=pb.ap[:],
                         l=wap[:, ic * 128:(ic + 1) * 128],
                         r=lnxt_ap(ctx, ic, tg * GW, GW),
                         s=(ic == 0), p=(ic == EC - 1):
                         e.matmul(o, l, r, start=s, stop=p),
                         reads=[ctx.LNXT.buf((ic, tg)), wbuf], writes=[pbuf],
                         tag=f"{tag}{nm}mm{h}_{tg}_{ic}")
                S.op("vector", lambda e, o=dstt[:, tg * GW:(tg + 1) * GW],
                     i=pb.ap[:], b=bcol[0][:, h:h + 1]:
                     e.tensor_scalar_add(o, i, b),
                     reads=[pbuf, bcol[1]], writes=[dbuf],
                     tag=f"{tag}{nm}ev{h}_{tg}")
        wap, wbuf = slabs["v"]
        for tc in range(TC):
            pb = ctx.PS[5 if tc % 2 == 0 else 6]
            pbuf = pb.buf()
            for ic in range(EC):
                S.op("tensor", lambda e, o=pb.ap[:, 0:128],
                     l=lnxt_ap(ctx, ic, tc * 128, 128),
                     r=wap[:, ic * 128:(ic + 1) * 128],
                     s=(ic == 0), p=(ic == EC - 1):
                     e.matmul(o, l, r, start=s, stop=p),
                     reads=[ctx.LNXT.buf((ic, tc // 4)), wbuf], writes=[pbuf],
                     tag=f"{tag}vmm{h}_{tc}_{ic}")
            S.op("scalar", lambda e, o=V[:, tc * 128:(tc + 1) * 128],
                 i=pb.ap[:, 0:128]: e.copy(o, i),
                 reads=[pbuf], writes=[v_buf], tag=f"{tag}vev{h}_{tc}")
        for g in range(TG):
            sc_list = list(range(0, 4 * g + 4))
            qr_buf = ctx.RZ.buf("qrow")
            qr = ctx.RZ.ap[64:65, 0:512]
            S.op("sync", lambda e, o=qr,
                 i=ctx.dram["c_alibi_qrow"][h * 4 + g:h * 4 + g + 1, :]:
                 e.dma_start(out=o, in_=i),
                 writes=[qr_buf], dma=True, tag=f"{tag}ldqr{h}_{g}", tiny=True)

            def gen(sc, bank, bank_buf, kt_buf=kt_buf, qt_buf=qt_buf, g=g,
                    qr_buf=qr_buf):
                S.op("tensor", lambda e, o=bank.ap[:], l=nrs,
                     r=ctx.RZ.ap[64:65, 0:512]:
                     e.matmul(o, l, r, start=True, stop=False),
                     reads=[qr_buf, nrs_b], writes=[bank_buf],
                     tag=f"{tag}sqr_{g}_{sc}")
                S.op("tensor", lambda e, o=bank.ap[:],
                     l=KT[:, sc * 128:(sc + 1) * 128],
                     r=QT[:, g * GW:(g + 1) * GW]:
                     e.matmul(o, l, r, start=False, stop=True),
                     reads=[kt_buf, qt_buf], writes=[bank_buf],
                     tag=f"{tag}smm_{g}_{sc}")

            def bias_fn(sc, h=h):
                idx = h * 16 + sc
                return ali[:, idx:idx + 1], ali_b

            def premask_fn(sc, g=g):
                p = sc - 4 * g
                if 0 <= p <= 3:
                    return msk[:, p * GW:(p + 1) * GW], msk_b
                return None

            def v_fn(sc, dc, v_buf=v_buf):
                return V[:, sc * 128:(sc + 1) * 128], v_buf

            pvb, pvbufs, z_buf, nseg = attn_core(
                ctx, tag, h, g, sc_list, gen, RS128, bias_fn, None, v_fn,
                1, 2, 0, premask_fn=premask_fn)
            rzb_ap, rzb_buf = softmax_norm(ctx, tag, h, g, z_buf, 0, 0, nseg)
            tf_ap, tf_buf = pv_merge(ctx, tag, h, g, pvb, pvbufs, nseg, 0,
                                     g % 2)
            S.op("vector", lambda e, o=tf_ap, a=tf_ap, b=rzb_ap:
                 e.tensor_mul(o, a, b),
                 reads=[tf_buf, rzb_buf], writes=[tf_buf], tag=f"{tag}n1{h}_{g}")
            o_buf = ctx.OUTS.buf(("o", g % 4))
            o_ap = ctx.OUTS.ap[:, (g % 4) * GW:(g % 4 + 1) * GW]
            S.op("vector", lambda e, o=o_ap, a=tf_ap, b=bias["bv"][0][:, h:h + 1]:
                 e.tensor_scalar_add(o, a, b),
                 reads=[tf_buf, bias["bv"][1]], writes=[o_buf],
                 tag=f"{tag}n2{h}_{g}")
            spill_store(ctx, tag, h * 128, g, o_ap, o_buf)

    emit_wo_residual(ctx, src_ap, src_buf, dst_ap, dst_buf,
                     ctx.dram["w_l0a_oT"], bias["bo"], tag)



def emit_ffn(ctx, src_ap, src_buf, dst_ap, dst_buf, sub):
    S = ctx.S
    is0 = sub.startswith("ffn0")
    tag = sub
    NIC = 44 if is0 else 64
    emit_ln(ctx, src_ap, src_buf, tag)
    ctx.WC.new_epoch()
    ctx.OUTS.new_epoch()
    ctx.TAB.new_epoch()
    spill = ctx.dram["spill"]
    if is0:
        gdram, udram = ctx.dram["w_f0_gT"], ctx.dram["w_f0_uT"]
        gs_b = ctx.MISC.buf((tag, "gs"))
        us_b = ctx.MISC.buf((tag, "us"))
        ds_b = ctx.MISC.buf((tag, "ds"))
        gs = ctx.MISC.ap[:, 384:428]
        us = ctx.MISC.ap[:, 428:472]
        ds = ctx.MISC.ap[:, 472:488]
        dram_load(ctx, "sync", gs, gs_b, ctx.dram["w_f0_gs"], f"{tag}ldgs")
        dram_load(ctx, "sync", us, us_b, ctx.dram["w_f0_us"], f"{tag}ldus")
        dram_load(ctx, "sync", ds, ds_b, ctx.dram["w_f0_ds"], f"{tag}ldds")
    else:
        gdram, udram = ctx.dram["w_f1_1T"], ctx.dram["w_f1_3T"]

    hst_buf = ctx.OUTS.buf("h")
    hst_ap = ctx.OUTS.ap
    for ic in range(NIC):
        gsl = ctx.WC.buf(("gslab", ic % 2))
        gsl_ap = ctx.WC.ap[:, (ic % 2) * 4096:(ic % 2) * 4096 + 2048]
        usl = ctx.WC.buf(("uslab", ic % 2))
        usl_ap = ctx.WC.ap[:, (ic % 2) * 4096 + 2048:(ic % 2) * 4096 + 4096]
        slab_load(ctx, gsl_ap, gsl, gdram, D, ic * 128, 128, f"{tag}ldg{ic}")
        slab_load(ctx, usl_ap, usl, udram, D, ic * 128, 128, f"{tag}ldu{ic}")
        for tg in range(TG):
            pg = ctx.PS[0 + (tg % 2)]
            pu = ctx.PS[2 + (tg % 2)]
            pg_buf, pu_buf = pg.buf(), pu.buf()
            for hc in range(EC):
                S.op("tensor", lambda e, o=pg.ap[:],
                     l=gsl_ap[:, hc * 128:(hc + 1) * 128],
                     r=lnxt_ap(ctx, hc, tg * GW, GW),
                     s=(hc == 0), p=(hc == EC - 1):
                     e.matmul(o, l, r, start=s, stop=p),
                     reads=[ctx.LNXT.buf((hc, tg)), gsl], writes=[pg_buf],
                     tag=f"{tag}gmm{ic}_{tg}_{hc}")
            for hc in range(EC):
                S.op("tensor", lambda e, o=pu.ap[:],
                     l=usl_ap[:, hc * 128:(hc + 1) * 128],
                     r=lnxt_ap(ctx, hc, tg * GW, GW),
                     s=(hc == 0), p=(hc == EC - 1):
                     e.matmul(o, l, r, start=s, stop=p),
                     reads=[ctx.LNXT.buf((hc, tg)), usl], writes=[pu_buf],
                     tag=f"{tag}umm{ic}_{tg}_{hc}")
            t1_buf = ctx.TMPF.buf(("fg", tg % 2))
            t1 = ctx.TMPF.ap[:, (tg % 2) * GW:(tg % 2) * GW + GW]
            t2_buf = ctx.TMPF.buf(("fu", tg % 2))
            t2 = ctx.TMPF.ap[:, (2 + tg % 2) * GW:(2 + tg % 2) * GW + GW]
            if is0:
                S.op("scalar", lambda e, o=t1, i=pg.ap[:], s=gs[:, ic:ic + 1]:
                     e.activation(o, i, AF.Silu, scale=s),
                     reads=[pg_buf, gs_b], writes=[t1_buf], tag=f"{tag}si{ic}_{tg}")
                S.op("vector", lambda e, o=t2, i=pu.ap[:], s=us[:, ic:ic + 1]:
                     e.tensor_scalar_mul(o, i, s),
                     reads=[pu_buf, us_b], writes=[t2_buf], tag=f"{tag}us{ic}_{tg}")
            else:
                S.op("scalar", lambda e, o=t1, i=pg.ap[:]:
                     e.activation(o, i, AF.Silu),
                     reads=[pg_buf], writes=[t1_buf], tag=f"{tag}si{ic}_{tg}")
                S.op("scalar", lambda e, o=t2, i=pu.ap[:]: e.copy(o, i),
                     reads=[pu_buf], writes=[t2_buf], tag=f"{tag}uc{ic}_{tg}")
            S.op("vector", lambda e, o=hst_ap[:, tg * GW:(tg + 1) * GW], a=t1,
                 b=t2: e.tensor_mul(o, a, b),
                 reads=[t1_buf, t2_buf], writes=[hst_buf], tag=f"{tag}h{ic}_{tg}")
        S.op("sync", lambda e, o=spill[ic * 128:(ic + 1) * 128, :],
             i=hst_ap[:, 0:T]: e.dma_start(out=o, in_=i),
             reads=[hst_buf], writes=[ctx.spill_buf], dma=True, tag=f"{tag}hs{ic}")

    panel = [(ctx.LNXT, 32), (ctx.WA, 8), (ctx.WB, 8), (ctx.WC, 8), (ctx.QK, 8)]
    for reg, _n in panel:
        reg.new_epoch()
    ctx.TMPF.new_epoch()
    ctx.XR.new_epoch()

    def ht(ic):
        a = ic
        for reg, n in panel:
            if a < n:
                return reg, a
            a -= n
        raise IndexError

    if is0:
        ddram = ctx.dram["w_f0_dT"]
    else:
        w2h = ctx.dram["w2hat"]
    banks = [[ctx.PS[0], ctx.PS[1]], [ctx.PS[2], ctx.PS[3]]]
    for tp in range(2):
        tt0 = tp * 1024
        start_ic = 0
        hp_bufs = {}
        for reg, n in panel:
            cnt = min(n, NIC - start_ic)
            if cnt <= 0:
                break
            b = reg.buf(("hp",))
            hp_bufs[id(reg)] = b
            S.op("sync", lambda e,
                 o=reg.ap[:, 0:cnt * 1024].rearrange("p (c n) -> p c n", c=cnt),
                 i=spill[start_ic * 128:(start_ic + cnt) * 128,
                         tt0:tt0 + 1024].rearrange("(c p) n -> p c n", p=128):
                 e.dma_start(out=o, in_=i),
                 reads=[ctx.spill_buf], writes=[b], dma=True,
                 tag=f"{tag}hl{tp}_{start_ic}")
            start_ic += cnt
        for op_ in range(8):
            bbufs = [[banks[j][th].buf() for th in range(2)] for j in range(2)]
            NSEG = NIC // 4
            for seg in range(NSEG):
                wt_i = seg % 2
                wt_buf = ctx.TAB.buf(("w2t", wt_i))
                wt_ap = ctx.TAB.ap[:, wt_i * 1024:(wt_i + 1) * 1024]
                wdram = ddram if is0 else w2h
                eng = "gpsimd" if is0 else "sync"
                S.op(eng, lambda e,
                     o=wt_ap.rearrange("p (c n) -> p c n", c=4),
                     i=wdram[seg * 512:(seg + 1) * 512,
                             op_ * 256:(op_ + 1) * 256].rearrange(
                                 "(c p) n -> p c n", p=128),
                     rd=None: e.dma_start(out=o, in_=i),
                     reads=([] if is0 else [ctx.w2hat_buf]), writes=[wt_buf],
                     dma=True, tag=f"{tag}wl{tp}_{op_}_{seg}")
                for j in range(2):
                    for th in range(2):
                        for icl in range(4):
                            ic = seg * 4 + icl
                            reg, slot = ht(ic)
                            hbuf = hp_bufs[id(reg)]
                            S.op("tensor", lambda e, o=banks[j][th].ap[:],
                                 l=wt_ap[:, icl * 256 + j * 128:
                                         icl * 256 + j * 128 + 128],
                                 r=reg.ap[:, slot * 1024 + th * GW:
                                          slot * 1024 + th * GW + GW],
                                 s=(ic == 0), p=(ic == NIC - 1):
                                 e.matmul(o, l, r, start=s, stop=p),
                                 reads=[hbuf, wt_buf], writes=[bbufs[j][th]],
                                 tag=f"{tag}om{tp}_{op_}_{ic}_{j}_{th}")
            for j in range(2):
                oc = op_ * 2 + j
                for th in range(2):
                    tg = tp * 2 + th
                    t0 = tg * GW
                    sl = (j * 2 + th) % 2
                    x_buf = ctx.XR.buf(("xres", sl))
                    x_ap = ctx.XR.ap[:, sl * GW:sl * GW + GW]
                    S.op("sync", lambda e, o=x_ap,
                         i=src_ap[oc * 128:(oc + 1) * 128, t0:t0 + GW]:
                         e.dma_start(out=o, in_=i),
                         reads=[src_buf], writes=[x_buf], dma=True,
                         tag=f"{tag}ox{tp}_{op_}_{j}_{th}")
                    o_buf = ctx.XR.buf(("xo", 2 + sl))
                    o_ap = ctx.XR.ap[:, (2 + sl) * GW:(2 + sl) * GW + GW]
                    if is0:
                        S.op("vector", lambda e, o=o_ap, a=banks[j][th].ap[:],
                             s=ds[:, oc:oc + 1], b=x_ap:
                             e.scalar_tensor_tensor(o, a, s, b, ALU.mult,
                                                    ALU.add),
                             reads=[bbufs[j][th], ds_b, x_buf], writes=[o_buf],
                             tag=f"{tag}oa{tp}_{op_}_{j}_{th}")
                    else:
                        S.op("vector", lambda e, o=o_ap, a=banks[j][th].ap[:],
                             b=x_ap: e.tensor_add(o, a, b),
                             reads=[bbufs[j][th], x_buf], writes=[o_buf],
                             tag=f"{tag}oa{tp}_{op_}_{j}_{th}")
                    S.op("sync", lambda e,
                         o=dst_ap[oc * 128:(oc + 1) * 128, t0:t0 + GW], i=o_ap:
                         e.dma_start(out=o, in_=i),
                         reads=[o_buf], writes=[dst_buf], dma=True,
                         tag=f"{tag}ot{tp}_{op_}_{j}_{th}")



def emit_w2hat(ctx):
    S = ctx.S
    tag = "w2h"
    load_consts(ctx)
    w2 = ctx.dram["w_f1_2T"]
    h_b = ctx.TAB.buf("h128")
    h_ap = ctx.TAB.ap[:, 0:128]
    dram_load(ctx, "sync", h_ap, h_b, ctx.dram["c_h128"], f"{tag}ldh")
    Y = ctx.LNXT
    for og in range(4):
        o0 = og * 512
        for c in range(64):
            sl_buf = ctx.WA.buf(("w2s", c % 2))
            sl_ap = ctx.WA.ap[:, (c % 2) * 512:(c % 2) * 512 + 512]
            S.op("gpsimd", lambda e, o=sl_ap,
                 i=w2[c * 128:(c + 1) * 128, o0:o0 + 512]:
                 e.dma_start(out=o, in_=i),
                 writes=[sl_buf], dma=True, tag=f"{tag}ls{og}_{c}")
            pb = ctx.PS[c % 2]
            pbuf = pb.buf()
            S.op("tensor", lambda e, o=pb.ap[:], l=h_ap, r=sl_ap:
                 e.matmul(o, l, r, start=True, stop=True),
                 reads=[sl_buf, h_b], writes=[pbuf], tag=f"{tag}mm{og}_{c}")
            yb = Y.buf(("y", c))
            S.op("scalar", lambda e, o=Y.ap[:, c * 512:(c + 1) * 512],
                 i=pb.ap[:]: e.copy(o, i),
                 reads=[pbuf], writes=[yb], tag=f"{tag}ev{og}_{c}")
        for st in range(6):
            stp = 1 << st
            for cbase in range(0, 64, stp * 2):
                for k in range(stp):
                    a = cbase + k
                    b = a + stp
                    ya_buf = Y.buf(("y", a))
                    yb_buf = Y.buf(("y", b))
                    ya = Y.ap[:, a * 512:(a + 1) * 512]
                    yb = Y.ap[:, b * 512:(b + 1) * 512]
                    eng = "vector" if k % 2 == 0 else "gpsimd"
                    sc_buf = ctx.OUTS.buf((tag, "scr", eng))
                    sc_ap = (ctx.OUTS.ap[:, 0:512] if eng == "vector"
                             else ctx.OUTS.ap[:, 512:1024])
                    S.op(eng, lambda e, o=sc_ap, x=ya, y=yb: e.tensor_sub(o, x, y),
                         reads=[ya_buf, yb_buf], writes=[sc_buf],
                         tag=f"{tag}b1_{og}_{st}_{a}")
                    S.op(eng, lambda e, o=ya, x=ya, y=yb: e.tensor_add(o, x, y),
                         reads=[ya_buf, yb_buf], writes=[ya_buf],
                         tag=f"{tag}b2_{og}_{st}_{a}")
                    S.op(eng, lambda e, o=yb, i=sc_ap: e.tensor_copy(o, i),
                         reads=[sc_buf], writes=[yb_buf],
                         tag=f"{tag}b3_{og}_{st}_{a}")
        ybufs = []
        for c in range(64):
            yb = Y.buf(("y", c))
            ybufs.append(yb)
            S.op("scalar", lambda e, o=Y.ap[:, c * 512:(c + 1) * 512],
                 i=Y.ap[:, c * 512:(c + 1) * 512]:
                 e.mul(o, i, SW2),
                 reads=[yb], writes=[yb], tag=f"{tag}sc{og}_{c}")
        S.op("sync", lambda e,
             o=ctx.dram["w2hat"][:, o0:o0 + 512].rearrange("(c p) n -> p c n",
                                                           p=128),
             i=Y.ap[:, 0:64 * 512].rearrange("p (c n) -> p c n", c=64):
             e.dma_start(out=o, in_=i),
             reads=ybufs, writes=[ctx.w2hat_buf], dma=True, tag=f"{tag}st{og}")
    Y.new_epoch()
    ctx.WA.new_epoch()
    ctx.TAB.new_epoch()
    ctx.OUTS.new_epoch()



def emit_l0b(ctx, src_ap, src_buf, dst_ap, dst_buf):
    S = ctx.S
    tag = "b0"
    emit_ln(ctx, src_ap, src_buf, tag)
    for r in (ctx.XQ, ctx.XR, ctx.QK, ctx.WA, ctx.WB, ctx.WC, ctx.TAB,
              ctx.PRB, ctx.PP):
        r.new_epoch()
    mf = ctx.MISC.ap
    cos_b = ctx.TAB.buf("cos")
    sin_b = ctx.TAB.buf("sin")
    msk_b = ctx.TAB.buf("mask")
    dram_load(ctx, "sync", ctx.TAB.ap[:, 0:2048], cos_b, ctx.dram["c_cos0b"],
              f"{tag}ldc")
    dram_load(ctx, "sync", ctx.TAB.ap[:, 2048:4096], sin_b, ctx.dram["c_sin0b"],
              f"{tag}lds")
    S.op("sync", lambda e: e.dma_start(
        out=ctx.TAB.ap[:, 4096:4096 + 4 * GW].rearrange("p (m n) -> p m n", m=4),
        in_=ctx.dram["c_mask_causal"].rearrange("m p n -> p m n")),
        writes=[msk_b], dma=True, tag=f"{tag}ldm")
    cosT = ctx.TAB.ap[:, 0:2048]
    sinT = ctx.TAB.ap[:, 2048:4096]
    mskT = ctx.TAB.ap[:, 4096:6144]
    dirs_b = ctx.MISB.buf("dirs")
    dirs = ctx.MISB.ap[0:3, 384:392]
    S.op("sync", lambda e: e.dma_start(out=dirs, in_=ctx.dram["c_dirs"]),
         writes=[dirs_b], dma=True, tag=f"{tag}lddirs")
    hs_b = ctx.MISC.buf("hsrow")
    hs_row = mf[0:1, 320:328]
    dram_load(ctx, "sync", hs_row, hs_b, ctx.dram["w_l0b_hs"], f"{tag}ldhs")
    of_ap, of_b = onesf(ctx)
    ob_ap, ob_b = onesb(ctx)
    pbh = ctx.PS[6]
    pbh_buf = pbh.buf()
    S.op("tensor", lambda e: e.matmul(pbh.ap[:, 0:8], of_ap[0:1, :], hs_row,
                                      start=True, stop=True),
         reads=[hs_b, of_b], writes=[pbh_buf], tag=f"{tag}hsbc")
    hsc_b = ctx.MISC.buf("hscol")
    hsc = mf[:, 328:336]
    S.op("scalar", lambda e: e.copy(hsc, pbh.ap[:, 0:8]),
         reads=[pbh_buf], writes=[hsc_b], tag=f"{tag}hscp")
    hs1_b = ctx.MISC.buf("hs1col")
    hs1 = mf[:, 336:344]
    S.op("vector", lambda e: e.tensor_scalar(hs1, hsc, -1.0, 1.0, ALU.mult,
                                             ALU.add),
         reads=[hsc_b], writes=[hs1_b], tag=f"{tag}hs1")

    wq, wk, wv = ctx.dram["w_l0b_q"], ctx.dram["w_l0b_k"], ctx.dram["w_l0b_v"]
    KT = ctx.WA.ap[:, 0:4096]
    V = ctx.WA.ap[:, 4096:8192]
    QT = ctx.QK.ap[:, 0:4096]
    qp_row = ctx.PP.ap[0:1, 0:2048]
    kp_row = ctx.PP.ap[32:33, 0:2048]

    def rope_cp(dst_ap, dst_buf, pbs, pbufs, tg, scr0, tagx):
        t0 = tg * GW
        s1_buf = ctx.QK.buf(("rs1", scr0))
        s1 = ctx.QK.ap[:, scr0:scr0 + GW]
        s2_buf = ctx.QK.buf(("rs2", scr0))
        s2 = ctx.QK.ap[:, scr0 + GW:scr0 + 2 * GW]
        S.op("vector", lambda e, o=s1, a=pbs[0].ap[:], b=cosT[:, t0:t0 + GW]:
             e.tensor_mul(o, a, b),
             reads=[pbufs[0], cos_b], writes=[s1_buf], tag=f"{tagx}r1")
        S.op("vector", lambda e, o=s2, a=pbs[1].ap[:], b=sinT[:, t0:t0 + GW]:
             e.tensor_mul(o, a, b),
             reads=[pbufs[1], sin_b], writes=[s2_buf], tag=f"{tagx}r2")
        S.op("vector", lambda e, o=dst_ap[:, t0:t0 + GW], a=s1, b=s2:
             e.tensor_sub(o, a, b),
             reads=[s1_buf, s2_buf], writes=[dst_buf], tag=f"{tagx}r3")
        S.op("vector", lambda e, o=s1, a=pbs[1].ap[:], b=cosT[:, t0:t0 + GW]:
             e.tensor_mul(o, a, b),
             reads=[pbufs[1], cos_b], writes=[s1_buf], tag=f"{tagx}r4")
        S.op("vector", lambda e, o=s2, a=pbs[0].ap[:], b=sinT[:, t0:t0 + GW]:
             e.tensor_mul(o, a, b),
             reads=[pbufs[0], sin_b], writes=[s2_buf], tag=f"{tagx}r5")
        S.op("vector", lambda e, o=dst_ap[:, 2048 + t0:2048 + t0 + GW], a=s1,
             b=s2: e.tensor_add(o, a, b),
             reads=[s1_buf, s2_buf], writes=[dst_buf], tag=f"{tagx}r6")

    for kv in range(2):
        ksl_buf = ctx.WB.buf("ks")
        ksl = ctx.WB.ap[:, 0:4096]
        slab_load(ctx, ksl, ksl_buf, wk, D, kv * 256, 256, f"{tag}lk{kv}")
        kt_buf = ctx.WA.buf(("kt", kv))
        for tg in range(TG):
            pbs = [ctx.PS[5], ctx.PS[6]]
            pbufs = [p.buf() for p in pbs]
            for c in range(2):
                for ic in range(EC):
                    S.op("tensor", lambda e, o=pbs[c].ap[:],
                         l=ksl[:, ic * 256 + c * 128:ic * 256 + c * 128 + 128],
                         r=lnxt_ap(ctx, ic, tg * GW, GW),
                         s=(ic == 0), p=(ic == EC - 1):
                         e.matmul(o, l, r, start=s, stop=p),
                         reads=[ctx.LNXT.buf((ic, tg)), ksl_buf],
                         writes=[pbufs[c]], tag=f"{tag}km{kv}_{tg}_{c}_{ic}")
            rope_cp(KT, kt_buf, pbs, pbufs, tg, 4096, f"{tag}kr{kv}_{tg}")
        vsl_buf = ctx.WB.buf("vs")
        vsl = ctx.WB.ap[:, 4096:8192]
        slab_load(ctx, vsl, vsl_buf, wv, D, kv * 256, 256, f"{tag}lv{kv}")
        v_buf = ctx.WA.buf(("v", kv))
        for tc in range(TC):
            pb2 = ctx.PS[5 if tc % 2 == 0 else 6]
            pbuf2 = pb2.buf()
            for ic in range(EC):
                S.op("tensor", lambda e, o=pb2.ap[:, 0:256],
                     l=lnxt_ap(ctx, ic, tc * 128, 128),
                     r=vsl[:, ic * 256:(ic + 1) * 256],
                     s=(ic == 0), p=(ic == EC - 1):
                     e.matmul(o, l, r, start=s, stop=p),
                     reads=[ctx.LNXT.buf((ic, tc // 4)), vsl_buf],
                     writes=[pbuf2], tag=f"{tag}vm{kv}_{tc}_{ic}")
            S.op("scalar", lambda e, o=V[:, tc * 256:(tc + 1) * 256],
                 i=pb2.ap[:, 0:256]: e.copy(o, i),
                 reads=[pbuf2], writes=[v_buf], tag=f"{tag}ve{kv}_{tc}")

        for hq in range(4):
            h = kv * 4 + hq
            qsl_buf = ctx.WC.buf(("qs", h % 2))
            qsl = ctx.WC.ap[:, (h % 2) * 4096:(h % 2) * 4096 + 4096]
            slab_load(ctx, qsl, qsl_buf, wq, D, h * 256, 256, f"{tag}lq{h}")
            qt_buf = ctx.QK.buf(("qt", h))
            for tg in range(TG):
                pbs = [ctx.PS[5], ctx.PS[6]]
                pbufs = [p.buf() for p in pbs]
                for c in range(2):
                    for ic in range(EC):
                        S.op("tensor", lambda e, o=pbs[c].ap[:],
                             l=qsl[:, ic * 256 + c * 128:ic * 256 + c * 128 + 128],
                             r=lnxt_ap(ctx, ic, tg * GW, GW),
                             s=(ic == 0), p=(ic == EC - 1):
                             e.matmul(o, l, r, start=s, stop=p),
                             reads=[ctx.LNXT.buf((ic, tg)), qsl_buf],
                             writes=[pbufs[c]], tag=f"{tag}qm{h}_{tg}_{c}_{ic}")
                rope_cp(QT, qt_buf, pbs, pbufs, tg, 4096 + 1024, f"{tag}qr{h}_{tg}")
            qp_buf = ctx.PP.buf("qp")
            kp_buf = ctx.PP.buf("kp")
            for tg in range(TG):
                pb3 = ctx.PS[5]
                pb3_buf = pb3.buf()
                S.op("tensor", lambda e, o=pb3.ap[0:1, :], l=dirs[:, h:h + 1],
                     r=QT[0:3, tg * GW:(tg + 1) * GW]:
                     e.matmul(o, l, r, start=True, stop=True),
                     reads=[qt_buf, dirs_b], writes=[pb3_buf],
                     tag=f"{tag}qp{h}_{tg}")
                S.op("scalar", lambda e, o=qp_row[:, tg * GW:(tg + 1) * GW],
                     i=pb3.ap[0:1, :]: e.copy(o, i),
                     reads=[pb3_buf], writes=[qp_buf], tag=f"{tag}qpc{h}_{tg}")
                pb4 = ctx.PS[6]
                pb4_buf = pb4.buf()
                S.op("tensor", lambda e, o=pb4.ap[0:1, :], l=dirs[:, h:h + 1],
                     r=KT[0:3, tg * GW:(tg + 1) * GW]:
                     e.matmul(o, l, r, start=True, stop=True),
                     reads=[kt_buf, dirs_b], writes=[pb4_buf],
                     tag=f"{tag}kp{h}_{tg}")
                S.op("scalar", lambda e, o=kp_row[:, tg * GW:(tg + 1) * GW],
                     i=pb4.ap[0:1, :]: e.copy(o, i),
                     reads=[pb4_buf], writes=[kp_buf], tag=f"{tag}kpc{h}_{tg}")
            kpc_b = ctx.MISC.buf(("kpc", h % 2))
            kpc = mf[:, 344 + (h % 2) * 16:360 + (h % 2) * 16]
            for sc in range(16):
                pb5 = ctx.PS[5]
                pb5_buf = pb5.buf()
                S.op("tensor", lambda e, o=pb5.ap[:, 0:1],
                     l=kp_row[:, sc * 128:(sc + 1) * 128],
                     r=ob_ap[32:33, 0:1]:
                     e.matmul(o, l, r, start=True, stop=True),
                     reads=[kp_buf, ob_b], writes=[pb5_buf],
                     tag=f"{tag}kc{h}_{sc}")
                S.op("scalar", lambda e, o=kpc[:, sc:sc + 1], i=pb5.ap[:, 0:1]:
                     e.copy(o, i),
                     reads=[pb5_buf], writes=[kpc_b], tag=f"{tag}kcc{h}_{sc}")

            for g in range(TG):
                sc_list = list(range(0, 4 * g + 4))
                pbq = ctx.PS[5]
                pbq_buf = pbq.buf()
                S.op("tensor", lambda e, o=pbq.ap[:], l=ob_ap[0:1, :],
                     r=qp_row[:, g * GW:(g + 1) * GW]:
                     e.matmul(o, l, r, start=True, stop=True),
                     reads=[qp_buf, ob_b], writes=[pbq_buf], tag=f"{tag}qb{h}_{g}")
                qpb_buf = ctx.TMPF.buf(("nrm", g % 2))
                qpb = ctx.TMPF.ap[:, (g % 2) * GW:(g % 2) * GW + GW]
                S.op("scalar", lambda e, o=qpb, i=pbq.ap[:]: e.copy(o, i),
                     reads=[pbq_buf], writes=[qpb_buf], tag=f"{tag}qbc{h}_{g}")

                def mask_fn(sc, g=g):
                    p = sc - 4 * g
                    if 0 <= p <= 3:
                        return mskT[:, p * GW:(p + 1) * GW], msk_b
                    return None

                def v_fn(sc, dc, v_buf=v_buf):
                    return (V[:, sc * 256 + dc * 128:sc * 256 + dc * 128 + 128],
                            v_buf)

                def genA(sc, bank, bank_buf, kt_buf=kt_buf, qt_buf=qt_buf, g=g):
                    for c in range(2):
                        S.op("tensor", lambda e, o=bank.ap[:],
                             l=KT[:, c * 2048 + sc * 128:c * 2048 + sc * 128 + 128],
                             r=QT[:, c * 2048 + g * GW:c * 2048 + g * GW + GW],
                             s=(c == 0), p=(c == 1):
                             e.matmul(o, l, r, start=s, stop=p),
                             reads=[kt_buf, qt_buf], writes=[bank_buf],
                             tag=f"{tag}sm{sc}_{c}")

                pvA, pvA_bufs, zA, nsA = attn_core(ctx, tag + "A", h, g,
                                                   sc_list, genA, RS256,
                                                   None, mask_fn, v_fn,
                                                   2, 2, 0)
                rzbA_ap, rzbA_buf = softmax_norm(ctx, tag + "A", h, g, zA,
                                                 0, 0, nsA)
                cA = []
                for dc in range(2):
                    ta_ap, ta_buf = pv_merge(ctx, tag + "A", h, g, pvA,
                                             pvA_bufs, nsA, dc, dc)
                    S.op("vector", lambda e, o=ta_ap, a=ta_ap, b=rzbA_ap:
                         e.tensor_mul(o, a, b),
                         reads=[ta_buf, rzbA_buf], writes=[ta_buf],
                         tag=f"{tag}nA{h}_{g}_{dc}")
                    cab = ctx.OUTS.buf(("cA", dc))
                    caa = ctx.OUTS.ap[:, (2 + dc) * GW:(3 + dc) * GW]
                    S.op("scalar", lambda e, o=caa, i=ta_ap: e.copy(o, i),
                         reads=[ta_buf], writes=[cab],
                         tag=f"{tag}cAc{h}_{g}_{dc}")
                    cA.append((caa, cab))
                ng = len(sc_list)
                nsG = (ng + 7) // 8
                zG_buf = ctx.PS[6].buf()
                pvG_banks = [[ctx.PS[2 + dc * nsG + m] for m in range(nsG)]
                             for dc in range(2)]
                pvG_bufs = [[b.buf() for b in row] for row in pvG_banks]
                for mseg in range(nsG):
                    segg = sc_list[mseg * 8:(mseg + 1) * 8]
                    gtiles = []
                    for k, sc in enumerate(segg):
                        pr_buf = ctx.PRB.buf(("pr", k))
                        pr_ap = ctx.PRB.ap[:, k * GW:(k + 1) * GW]
                        S.op("scalar", lambda e, o=pr_ap, i=qpb,
                             s=kpc[:, sc:sc + 1]:
                             e.activation(o, i, AF.Exp, scale=s),
                             reads=[qpb_buf, kpc_b], writes=[pr_buf],
                             tag=f"{tag}gx{h}_{g}_{sc}")
                        mask = mask_fn(sc)
                        if mask is not None:
                            S.op("vector", lambda e, o=pr_ap, a=pr_ap,
                                 b=mask[0]: e.tensor_mul(o, a, b),
                                 reads=[pr_buf, mask[1]], writes=[pr_buf],
                                 tag=f"{tag}gm{h}_{g}_{sc}")
                        gtiles.append((sc, pr_ap, pr_buf))
                    zr = 32 * mseg
                    for k, (sc, pr_ap, pr_buf) in enumerate(gtiles):
                        S.op("tensor", lambda e, o=ctx.PS[6].ap[zr:zr + 1, :],
                             l=ob_ap[:, 0:1], r=pr_ap, s=(k == 0),
                             p=(k == len(gtiles) - 1):
                             e.matmul(o, l, r, start=s, stop=p),
                             reads=[pr_buf, ob_b], writes=[zG_buf],
                             tag=f"{tag}gz{h}_{g}_{sc}")
                    for dc in range(2):
                        for k, (sc, pr_ap, pr_buf) in enumerate(gtiles):
                            vap, vbuf = v_fn(sc, dc)
                            S.op("tensor", lambda e,
                                 o=pvG_banks[dc][mseg].ap[:], l=vap, r=pr_ap,
                                 s=(k == 0), p=(k == len(gtiles) - 1):
                                 e.matmul(o, l, r, start=s, stop=p),
                                 reads=[pr_buf, vbuf],
                                 writes=[pvG_bufs[dc][mseg]],
                                 tag=f"{tag}gp{h}_{g}_{sc}_{dc}")
                rzbG_ap, rzbG_buf = softmax_norm(ctx, tag + "G", h, g, zG_buf,
                                                 0, 1, nsG)
                for dc in range(2):
                    tg_ap, tg_buf = pv_merge(ctx, tag + "G", h, g, pvG_banks,
                                             pvG_bufs, nsG, dc, dc)
                    S.op("vector", lambda e, o=tg_ap, a=tg_ap, b=rzbG_ap:
                         e.tensor_mul(o, a, b),
                         reads=[tg_buf, rzbG_buf], writes=[tg_buf],
                         tag=f"{tag}nG{h}_{g}_{dc}")
                    caa, cab = cA[dc]
                    S.op("gpsimd", lambda e, o=caa, a=caa, s=hs1[:, h:h + 1]:
                         e.tensor_scalar_mul(o, a, s),
                         reads=[cab, hs1_b], writes=[cab],
                         tag=f"{tag}c1{h}_{g}_{dc}")
                    o_buf = ctx.OUTS.buf(("o", dc))
                    o_ap = ctx.OUTS.ap[:, dc * GW:(dc + 1) * GW]
                    S.op("vector", lambda e, o=o_ap, a=tg_ap,
                         s=hsc[:, h:h + 1], b=caa:
                         e.scalar_tensor_tensor(o, a, s, b, ALU.mult, ALU.add),
                         reads=[tg_buf, hsc_b, cab], writes=[o_buf],
                         tag=f"{tag}cf{h}_{g}_{dc}")
                    spill_store(ctx, tag, h * 256 + dc * 128, g, o_ap, o_buf)

    emit_wo_residual(ctx, src_ap, src_buf, dst_ap, dst_buf,
                     ctx.dram["w_l0b_o"], None, tag)



def emit_l1a(ctx, src_ap, src_buf, dst_ap, dst_buf):
    S = ctx.S
    tag = "a1"
    emit_ln(ctx, src_ap, src_buf, tag)
    for r in (ctx.XQ, ctx.XR, ctx.QK, ctx.WC, ctx.TAB, ctx.PRB):
        r.new_epoch()
    cos_b = ctx.TAB.buf("cos")
    sin_b = ctx.TAB.buf("sin")
    msk_b = ctx.TAB.buf("mask")
    dram_load(ctx, "sync", ctx.TAB.ap[:, 0:2048], cos_b, ctx.dram["c_cos1"],
              f"{tag}ldc")
    dram_load(ctx, "sync", ctx.TAB.ap[:, 2048:4096], sin_b, ctx.dram["c_sin1s"],
              f"{tag}lds")
    S.op("sync", lambda e: e.dma_start(
        out=ctx.TAB.ap[:, 4096:8192].rearrange("p (m n) -> p m n", m=8),
        in_=ctx.dram["c_mask_l1a"].rearrange("m p n -> p m n")),
        writes=[msk_b], dma=True, tag=f"{tag}ldm")
    cosT = ctx.TAB.ap[:, 0:2048]
    sinT = ctx.TAB.ap[:, 2048:4096]
    mskT = ctx.TAB.ap[:, 4096:8192]

    wq, wk, wv = ctx.dram["w_l1a_qT"], ctx.dram["w_l1a_kT"], ctx.dram["w_l1a_vT"]
    QT = ctx.QK.ap[:, 0:2048]
    KT = ctx.QK.ap[:, 2048:4096]
    V = ctx.QK.ap[:, 4096:6144]
    SCR = ctx.QK.ap[:, 6144:8192]

    def rope_half(dstt, dst_buf, pb, pbuf, tg, tagx):
        t0 = tg * GW
        if L1A_SKIP_ROPE or L1A_ROPE_LEVEL == 0:
            S.op("scalar", lambda e, o=dstt[:, t0:t0 + GW], i=pb.ap[:]:
                 e.copy(o, i),
                 reads=[pbuf], writes=[dst_buf], tag=f"{tagx}nr")
            return
        sw_buf = ctx.QK.buf("rsw")
        sw = SCR[:, 0:GW]
        t1_buf = ctx.QK.buf("rt1")
        t1 = SCR[:, GW:2 * GW]
        S.op("scalar", lambda e, o=sw[0:64, :], i=pb.ap[64:128, :]: e.copy(o, i),
             reads=[pbuf], writes=[sw_buf], tag=f"{tagx}s1")
        S.op("scalar", lambda e, o=sw[64:128, :], i=pb.ap[0:64, :]: e.copy(o, i),
             reads=[pbuf], writes=[sw_buf], tag=f"{tagx}s2")
        if L1A_ROPE_LEVEL == 1:
            S.op("scalar", lambda e, o=dstt[:, t0:t0 + GW], i=pb.ap[:]:
                 e.copy(o, i),
                 reads=[pbuf, sw_buf], writes=[dst_buf], tag=f"{tagx}nr")
            return
        S.op("vector", lambda e, o=t1, a=pb.ap[:], b=cosT[:, t0:t0 + GW]:
             e.tensor_mul(o, a, b),
             reads=[pbuf, cos_b, sw_buf], writes=[t1_buf], tag=f"{tagx}t1")
        if L1A_ROPE_LEVEL == 2:
            S.op("scalar", lambda e, o=dstt[:, t0:t0 + GW], i=t1:
                 e.copy(o, i),
                 reads=[t1_buf, sw_buf], writes=[dst_buf], tag=f"{tagx}nr")
            return
        S.op("vector", lambda e, o=sw, a=sw, b=sinT[:, t0:t0 + GW]:
             e.tensor_mul(o, a, b),
             reads=[sw_buf, sin_b], writes=[sw_buf], tag=f"{tagx}t2")
        S.op("vector", lambda e, o=dstt[:, t0:t0 + GW], a=t1, b=sw:
             e.tensor_add(o, a, b),
             reads=[t1_buf, sw_buf], writes=[dst_buf], tag=f"{tagx}t3")

    for h in range(L1A_NHEADS):
        slabs = {}
        for si, (nm, wt) in enumerate((("q", wq), ("k", wk), ("v", wv))):
            sl = (h * 3 + si) % 4
            wbuf = ctx.WC.buf(("slab", sl))
            wap = ctx.WC.ap[:, sl * 2048:(sl + 1) * 2048]
            slab_load(ctx, wap, wbuf, wt, D, h * 128, 128, f"{tag}ls{h}{nm}")
            slabs[nm] = (wap, wbuf)
        qt_buf = ctx.QK.buf(("qt", h))
        kt_buf = ctx.QK.buf(("kt", h))
        v_buf = ctx.QK.buf(("v", h))
        for nm, dstt, dbuf in (("q", QT, qt_buf), ("k", KT, kt_buf)):
            wap, wbuf = slabs[nm]
            for tg in range(TG):
                pb = ctx.PS[5 if tg % 2 == 0 else 6]
                pbuf = pb.buf()
                for ic in range(EC):
                    S.op("tensor", lambda e, o=pb.ap[:],
                         l=wap[:, ic * 128:(ic + 1) * 128],
                         r=lnxt_ap(ctx, ic, tg * GW, GW),
                         s=(ic == 0), p=(ic == EC - 1):
                         e.matmul(o, l, r, start=s, stop=p),
                         reads=[ctx.LNXT.buf((ic, tg)), wbuf], writes=[pbuf],
                         tag=f"{tag}{nm}m{h}_{tg}_{ic}")
                rope_half(dstt, dbuf, pb, pbuf, tg, f"{tag}{nm}r{h}_{tg}")
        wap, wbuf = slabs["v"]
        for tc in range(TC):
            pb = ctx.PS[5 if tc % 2 == 0 else 6]
            pbuf = pb.buf()
            for ic in range(EC):
                S.op("tensor", lambda e, o=pb.ap[:, 0:128],
                     l=lnxt_ap(ctx, ic, tc * 128, 128),
                     r=wap[:, ic * 128:(ic + 1) * 128],
                     s=(ic == 0), p=(ic == EC - 1):
                     e.matmul(o, l, r, start=s, stop=p),
                     reads=[ctx.LNXT.buf((ic, tc // 4)), wbuf], writes=[pbuf],
                     tag=f"{tag}vm{h}_{tc}_{ic}")
            S.op("scalar", lambda e, o=V[:, tc * 128:(tc + 1) * 128],
                 i=pb.ap[:, 0:128]: e.copy(o, i),
                 reads=[pbuf], writes=[v_buf], tag=f"{tag}ve{h}_{tc}")
        for g in range(TG):
            sc_list = list(range(max(0, 4 * g - 4), 4 * g + 4))

            def gen(sc, bank, bank_buf, kt_buf=kt_buf, qt_buf=qt_buf, g=g):
                S.op("tensor", lambda e, o=bank.ap[:],
                     l=KT[:, sc * 128:(sc + 1) * 128],
                     r=QT[:, g * GW:(g + 1) * GW]:
                     e.matmul(o, l, r, start=True, stop=True),
                     reads=[kt_buf, qt_buf], writes=[bank_buf],
                     tag=f"{tag}sm{h}_{g}_{sc}")

            def mask_fn(sc, g=g):
                if L1A_SKIP_MASK:
                    return None
                r = sc - (4 * g - 4)
                return mskT[:, r * GW:(r + 1) * GW], msk_b

            def v_fn(sc, dc, v_buf=v_buf):
                return V[:, sc * 128:(sc + 1) * 128], v_buf

            pvb, pvbufs, z_buf, nseg = attn_core(ctx, tag, h, g, sc_list,
                                                 gen, RS128, None, mask_fn,
                                                 v_fn, 1, 2, 0)
            rzb_ap, rzb_buf = softmax_norm(ctx, tag, h, g, z_buf, 0, 0, nseg)
            tf_ap, tf_buf = pv_merge(ctx, tag, h, g, pvb, pvbufs, nseg, 0,
                                     g % 2)
            o_buf = ctx.OUTS.buf(("o", g % 4))
            o_ap = ctx.OUTS.ap[:, (g % 4) * GW:(g % 4 + 1) * GW]
            S.op("vector", lambda e, o=o_ap, a=tf_ap, b=rzb_ap:
                 e.tensor_mul(o, a, b),
                 reads=[tf_buf, rzb_buf], writes=[o_buf], tag=f"{tag}n{h}_{g}")
            spill_store(ctx, tag, h * 128, g, o_ap, o_buf)

    emit_wo_residual(ctx, src_ap, src_buf, dst_ap, dst_buf,
                     ctx.dram["w_l1a_oT"], None, tag)



def emit_l1b(ctx, src_ap, src_buf, dst_ap, dst_buf):
    S = ctx.S
    tag = "b1"
    emit_ln(ctx, src_ap, src_buf, tag)
    for r in (ctx.XQ, ctx.XR, ctx.QK, ctx.WB, ctx.WC, ctx.TAB, ctx.PRB,
              ctx.PP, ctx.SMF):
        r.new_epoch()
    of_ap, of_b = onesf(ctx)
    ob_ap, ob_b = onesb(ctx)
    mf = ctx.MISC.ap
    cos_b = ctx.TAB.buf("cos")
    sin_b = ctx.TAB.buf("sin")
    msk_b = ctx.TAB.buf("mask")
    dram_load(ctx, "sync", ctx.TAB.ap[:, 0:2048], cos_b, ctx.dram["c_cos1"],
              f"{tag}ldc")
    dram_load(ctx, "sync", ctx.TAB.ap[:, 2048:4096], sin_b, ctx.dram["c_sin1s"],
              f"{tag}lds")
    S.op("sync", lambda e: e.dma_start(
        out=ctx.TAB.ap[:, 4096:6144].rearrange("p (m n) -> p m n", m=4),
        in_=ctx.dram["c_mask_causal"].rearrange("m p n -> p m n")),
        writes=[msk_b], dma=True, tag=f"{tag}ldm")
    cosT = ctx.TAB.ap[:, 0:2048]
    sinT = ctx.TAB.ap[:, 2048:4096]
    mskT = ctx.TAB.ap[:, 4096:6144]
    scrB = ctx.TAB.ap[:, 6144:8192]
    ident_b = ctx.MISB.buf("ident")
    ident = ctx.MISB.ap[:, 256:384]
    dram_load(ctx, "sync", ident, ident_b, ctx.dram["c_ident_bf"], f"{tag}ldid")
    trf_b = ctx.MISC.buf("triu_f")
    trf = mf[:, 128:256]
    dram_load(ctx, "sync", trf, trf_b, ctx.dram["c_triu_f32"], f"{tag}ldtf")
    idf_b = ctx.MISC.buf("ident_f")
    idf = mf[:, 520:648]
    dram_load(ctx, "sync", idf, idf_b, ctx.dram["c_ident_f32"], f"{tag}ldif")
    nrs_b = ctx.MISC.buf("nrsinv")
    nrs = mf[64:65, 648:776]
    dram_load(ctx, "sync", mf[:, 648:776], nrs_b, ctx.dram["c_nrsinv"],
              f"{tag}ldnr")
    bf_b = ctx.MISC.buf("bfrow")
    bf_row = mf[0:1, 488:504]
    dram_load(ctx, "sync", bf_row, bf_b, ctx.dram["w_l1b_bf"], f"{tag}ldbf")
    pbb = ctx.PS[6]
    pbb_buf = pbb.buf()
    S.op("tensor", lambda e: e.matmul(pbb.ap[:, 0:16], of_ap[0:1, :], bf_row,
                                      start=True, stop=True),
         reads=[bf_b, of_b], writes=[pbb_buf], tag=f"{tag}bfb")
    bfb_b = ctx.MISC.buf("bfb")
    bfb = mf[:, 504:520]
    S.op("scalar", lambda e: e.copy(bfb, pbb.ap[:, 0:16]),
         reads=[pbb_buf], writes=[bfb_b], tag=f"{tag}bfc")

    for fi, (fl, wsname) in enumerate((("q", "w_l1b_sqT"), ("k", "w_l1b_skT"),
                                       ("v", "w_l1b_svT"))):
        ws = ctx.dram[wsname]
        wsl_b = ctx.WB.buf(("ws", fi))
        wsl = ctx.WB.ap[:, fi * 256:(fi + 1) * 256]
        slab_load(ctx, wsl, wsl_b, ws, D, 0, 16, f"{tag}lw{fl}")
        p_buf = ctx.PP.buf(("p", fl))
        for tc in range(TC):
            pb = ctx.PS[5 if tc % 2 == 0 else 6]
            pbuf = pb.buf()
            for ic in range(EC):
                S.op("tensor", lambda e, o=pb.ap[:, 0:16],
                     l=lnxt_ap(ctx, ic, tc * 128, 128),
                     r=wsl[:, ic * 16:(ic + 1) * 16],
                     s=(ic == 0), p=(ic == EC - 1):
                     e.matmul(o, l, r, start=s, stop=p),
                     reads=[ctx.LNXT.buf((ic, tc // 4)), wsl_b], writes=[pbuf],
                     tag=f"{tag}pm{fl}_{tc}_{ic}")
            S.op("scalar", lambda e,
                 o=ctx.PP.ap[:, fi * 256 + tc * 16:fi * 256 + (tc + 1) * 16],
                 i=pb.ap[:, 0:16]: e.activation(o, i, AF.Sigmoid),
                 reads=[pbuf], writes=[p_buf], tag=f"{tag}pe{fl}_{tc}")
    wf = ctx.dram["w_l1b_fT"]
    wfl_b = ctx.WB.buf(("ws", 3))
    wfl = ctx.WB.ap[:, 768:1024]
    slab_load(ctx, wfl, wfl_b, wf, D, 0, 16, f"{tag}lwf")
    logf_buf = ctx.SMF.buf("logf")
    for tc in range(TC):
        pb = ctx.PS[5 if tc % 2 == 0 else 6]
        pbuf = pb.buf()
        for ic in range(EC):
            S.op("tensor", lambda e, o=pb.ap[:, 0:16],
                 l=lnxt_ap(ctx, ic, tc * 128, 128),
                 r=wfl[:, ic * 16:(ic + 1) * 16],
                 s=(ic == 0), p=(ic == EC - 1):
                 e.matmul(o, l, r, start=s, stop=p),
                 reads=[ctx.LNXT.buf((ic, tc // 4)), wfl_b], writes=[pbuf],
                 tag=f"{tag}fm{tc}_{ic}")
        za_buf = ctx.TMPF.buf(("z", tc % 2))
        za = ctx.TMPF.ap[:, (tc % 2) * GW:(tc % 2) * GW + 16]
        S.op("vector", lambda e, o=za, a=pb.ap[:, 0:16], b=bfb:
             e.tensor_add(o, a, b),
             reads=[pbuf, bfb_b], writes=[za_buf], tag=f"{tag}fz{tc}")
        ab_buf = ctx.TMPF.buf(("fa", tc % 2))
        ab = ctx.TMPF.ap[:, (2 + tc % 2) * GW:(2 + tc % 2) * GW + 16]
        S.op("scalar", lambda e, o=ab, i=za: e.activation(o, i, AF.Abs),
             reads=[za_buf], writes=[ab_buf], tag=f"{tag}fb{tc}")
        S.op("scalar", lambda e, o=ab, i=ab: e.activation(o, i, AF.Exp,
                                                          scale=-1.0),
             reads=[ab_buf], writes=[ab_buf], tag=f"{tag}fe{tc}")
        S.op("scalar", lambda e, o=ab, i=ab: e.activation(o, i, AF.Ln, bias=1.0),
             reads=[ab_buf], writes=[ab_buf], tag=f"{tag}fl{tc}")
        S.op("vector", lambda e, o=za, a=za: e.tensor_scalar_min(o, a, 0.0),
             reads=[za_buf], writes=[za_buf], tag=f"{tag}fn{tc}")
        S.op("vector", lambda e, o=ctx.SMF.ap[:, tc * 16:(tc + 1) * 16],
             a=ab, b=za: e.tensor_sub(o, a, b),
             reads=[za_buf, ab_buf], writes=[logf_buf], tag=f"{tag}ff{tc}")
    ncc_buf = ctx.XR.buf("ncc")
    ncc = ctx.XR.ap[:, 3 * GW:3 * GW + 256]
    for half in range(2):
        pb = ctx.PS[5]
        pbuf = pb.buf()
        for sc in range(half * 8, half * 8 + 8):
            for jc in range(0, sc + 1):
                lhs, lb = (trf, trf_b) if jc == sc else (of_ap, of_b)
                S.op("tensor", lambda e,
                     o=pb.ap[:, (sc - half * 8) * 16:(sc - half * 8 + 1) * 16],
                     l=lhs, r=ctx.SMF.ap[:, jc * 16:(jc + 1) * 16],
                     s=(jc == 0), p=(jc == sc):
                     e.matmul(o, l, r, start=s, stop=p),
                     reads=[logf_buf, lb], writes=[pbuf],
                     tag=f"{tag}nc{sc}_{jc}")
        S.op("scalar", lambda e, o=ncc[:, half * 128:half * 128 + 128],
             i=pb.ap[:, 0:128]: e.copy(o, i),
             reads=[pbuf], writes=[ncc_buf], tag=f"{tag}nce{half}")
    wqd, wkd, wvd = (ctx.dram["w_l1b_qT"], ctx.dram["w_l1b_kT"],
                     ctx.dram["w_l1b_vT"])
    QT = ctx.QK.ap[:, 0:2048]
    KT = ctx.QK.ap[:, 2048:4096]
    V = ctx.QK.ap[:, 4096:6144]
    scrA = ctx.QK.ap[:, 6144:8192]
    prow = ctx.PP.ap[0:1, 768:2816]
    ctx.TMPF.new_epoch()

    for h in range(16):
        for fi, (fl, wd) in enumerate((("q", wqd), ("k", wkd), ("v", wvd))):
            sl = (h * 3 + fi) % 2
            wbuf = ctx.WC.buf(("slab", sl))
            wap = ctx.WC.ap[:, sl * 2048:(sl + 1) * 2048]
            slab_load(ctx, wap, wbuf, wd, D, h * 128, 128, f"{tag}l{fl}{h}")
            y_buf = ctx.QK.buf("scrA")
            sB_buf = ctx.TAB.buf("scrB")
            for tg in range(TG):
                pb = ctx.PS[5 if tg % 2 == 0 else 6]
                pbuf = pb.buf()
                for ic in range(EC):
                    S.op("tensor", lambda e, o=pb.ap[:],
                         l=wap[:, ic * 128:(ic + 1) * 128],
                         r=lnxt_ap(ctx, ic, tg * GW, GW),
                         s=(ic == 0), p=(ic == EC - 1):
                         e.matmul(o, l, r, start=s, stop=p),
                         reads=[ctx.LNXT.buf((ic, tg)), wbuf], writes=[pbuf],
                         tag=f"{tag}{fl}m{h}_{tg}_{ic}")
                S.op("scalar", lambda e, o=scrA[:, tg * GW:(tg + 1) * GW],
                     i=pb.ap[:]: e.copy(o, i),
                     reads=[pbuf], writes=[y_buf], tag=f"{tag}{fl}e{h}_{tg}")
            S.op("vector", lambda e, o=scrB[:, 1:2048], a=scrA[:, 1:2048],
                 b=scrA[:, 0:2047]: e.tensor_sub(o, a, b),
                 reads=[y_buf], writes=[sB_buf], tag=f"{tag}{fl}d{h}")
            S.op("scalar", lambda e, o=scrB[:, 0:1], i=scrA[:, 0:1]: e.copy(o, i),
                 reads=[y_buf], writes=[sB_buf], tag=f"{tag}{fl}d0{h}")
            prow_buf = ctx.PP.buf(("prow",))
            for tc in range(TC):
                pbt = ctx.PS[6]
                pbt_buf = pbt.buf()
                S.op("tensor", lambda e, o=pbt.ap[0:1, 0:128],
                     l=ctx.PP.ap[:, fi * 256 + tc * 16 + h:
                                 fi * 256 + tc * 16 + h + 1],
                     r=ident: e.matmul(o, l, r, start=True, stop=True),
                     reads=[ctx.PP.buf(("p", fl)), ident_b], writes=[pbt_buf],
                     tag=f"{tag}{fl}pt{h}_{tc}")
                S.op("scalar", lambda e, o=prow[:, tc * 128:(tc + 1) * 128],
                     i=pbt.ap[0:1, 0:128]: e.copy(o, i),
                     reads=[pbt_buf], writes=[prow_buf], tag=f"{tag}{fl}pc{h}_{tc}")
            dstt = {"q": QT, "k": KT, "v": None}[fl]
            dst_b = (ctx.QK.buf(("qt", h)) if fl == "q" else
                     ctx.QK.buf(("kt", h)) if fl == "k" else sB_buf)
            for tg in range(TG):
                t0 = tg * GW
                pbp = ctx.PS[5]
                pbp_buf = pbp.buf()
                S.op("tensor", lambda e, o=pbp.ap[:], l=ob_ap[0:1, :],
                     r=prow[:, t0:t0 + GW]:
                     e.matmul(o, l, r, start=True, stop=True),
                     reads=[prow_buf, ob_b], writes=[pbp_buf],
                     tag=f"{tag}{fl}pb{h}_{tg}")
                tm_buf = ctx.TMPF.buf(("mx", tg % 2))
                tm = ctx.TMPF.ap[:, (tg % 2) * GW:(tg % 2) * GW + GW]
                S.op("vector", lambda e, o=tm, a=scrB[:, t0:t0 + GW],
                     b=pbp.ap[:]: e.tensor_mul(o, a, b),
                     reads=[sB_buf, pbp_buf], writes=[tm_buf],
                     tag=f"{tag}{fl}x1{h}_{tg}")
                if fl == "v":
                    S.op("vector", lambda e, o=scrB[:, t0:t0 + GW],
                         a=scrA[:, t0:t0 + GW], b=tm: e.tensor_sub(o, a, b),
                         reads=[y_buf, tm_buf], writes=[sB_buf],
                         tag=f"{tag}{fl}x2{h}_{tg}")
                else:
                    S.op("vector", lambda e, o=dstt[:, t0:t0 + GW],
                         a=scrA[:, t0:t0 + GW], b=tm: e.tensor_sub(o, a, b),
                         reads=[y_buf, tm_buf], writes=[dst_b],
                         tag=f"{tag}{fl}x2{h}_{tg}")
            if fl == "v":
                v_buf = ctx.QK.buf(("v", h))
                for tc in range(TC):
                    pt_buf = ctx.PST.buf()
                    S.op("tensor", lambda e, o=ctx.PST.ap[:, 0:128],
                         i=scrB[:, tc * 128:(tc + 1) * 128], idn=ident:
                         e.transpose(o, i, idn),
                         reads=[sB_buf, ident_b], writes=[pt_buf],
                         tag=f"{tag}vt{h}_{tc}")
                    S.op("scalar", lambda e, o=V[:, tc * 128:(tc + 1) * 128],
                         i=ctx.PST.ap[:, 0:128]: e.copy(o, i),
                         reads=[pt_buf], writes=[v_buf], tag=f"{tag}vc{h}_{tc}")
            else:
                for tg in range(TG):
                    t0 = tg * GW
                    sw = scrB[:, 0:GW]
                    t1_buf = ctx.PRB.buf(("pr", 6 + tg % 2))
                    t1 = ctx.PRB.ap[:, (6 + tg % 2) * GW:(7 + tg % 2) * GW]
                    S.op("scalar", lambda e, o=sw[0:64, :],
                         i=dstt[64:128, t0:t0 + GW]: e.copy(o, i),
                         reads=[dst_b], writes=[sB_buf], tag=f"{tag}{fl}w1{h}_{tg}")
                    S.op("scalar", lambda e, o=sw[64:128, :],
                         i=dstt[0:64, t0:t0 + GW]: e.copy(o, i),
                         reads=[dst_b], writes=[sB_buf], tag=f"{tag}{fl}w2{h}_{tg}")
                    S.op("vector", lambda e, o=t1, a=dstt[:, t0:t0 + GW],
                         b=cosT[:, t0:t0 + GW]: e.tensor_mul(o, a, b),
                         reads=[dst_b, cos_b], writes=[t1_buf],
                         tag=f"{tag}{fl}c1{h}_{tg}")
                    S.op("vector", lambda e, o=sw, a=sw, b=sinT[:, t0:t0 + GW]:
                         e.tensor_mul(o, a, b),
                         reads=[sB_buf, sin_b], writes=[sB_buf],
                         tag=f"{tag}{fl}c2{h}_{tg}")
                    S.op("vector", lambda e, o=dstt[:, t0:t0 + GW], a=t1, b=sw:
                         e.tensor_sub(o, a, b),
                         reads=[t1_buf, sB_buf], writes=[dst_b],
                         tag=f"{tag}{fl}c3{h}_{tg}")
        qt_buf = ctx.QK.buf(("qt", h))
        kt_buf = ctx.QK.buf(("kt", h))
        v_buf = ctx.QK.buf(("v", h))
        for g in range(TG):
            sc_list = list(range(0, 4 * g + 4))
            qr_buf = ctx.RZ.buf("ncr")
            for tq in range(4):
                tc = 4 * g + tq
                pbt = ctx.PS[5]
                pbt_buf = pbt.buf()
                S.op("tensor", lambda e, o=pbt.ap[0:1, 0:128],
                     l=ncc[:, tc * 16 + h:tc * 16 + h + 1], r=idf:
                     e.matmul(o, l, r, start=True, stop=True),
                     reads=[ncc_buf, idf_b], writes=[pbt_buf],
                     tag=f"{tag}nr{h}_{g}_{tq}")
                S.op("scalar", lambda e,
                     o=ctx.RZ.ap[64:65, tq * 128:(tq + 1) * 128],
                     i=pbt.ap[0:1, 0:128]: e.copy(o, i),
                     reads=[pbt_buf], writes=[qr_buf], tag=f"{tag}nrc{h}_{g}_{tq}")

            def gen(sc, bank, bank_buf, kt_buf=kt_buf, qt_buf=qt_buf, g=g,
                    qr_buf=qr_buf):
                S.op("tensor", lambda e, o=bank.ap[:],
                     l=KT[:, sc * 128:(sc + 1) * 128],
                     r=QT[:, g * GW:(g + 1) * GW]:
                     e.matmul(o, l, r, start=True, stop=False),
                     reads=[kt_buf, qt_buf], writes=[bank_buf],
                     tag=f"{tag}sm{h}_{g}_{sc}")
                S.op("tensor", lambda e, o=bank.ap[:], l=nrs,
                     r=ctx.RZ.ap[64:65, 0:512]:
                     e.matmul(o, l, r, start=False, stop=True),
                     reads=[qr_buf, nrs_b], writes=[bank_buf],
                     tag=f"{tag}sq{h}_{g}_{sc}")

            def bias_fn(sc, h=h):
                idx = sc * 16 + h
                return ncc[:, idx:idx + 1], ncc_buf

            def premask_fn(sc, g=g):
                p = sc - 4 * g
                if 0 <= p <= 3:
                    return mskT[:, p * GW:(p + 1) * GW], msk_b
                return None

            def v_fn(sc, dc, v_buf=v_buf):
                return V[:, sc * 128:(sc + 1) * 128], v_buf

            pvb, pvbufs, z_buf, nseg = attn_core(
                ctx, tag, h, g, sc_list, gen, RS128, bias_fn, None, v_fn,
                1, 2, 0, premask_fn=premask_fn)
            rzb_ap, rzb_buf = softmax_norm(ctx, tag, h, g, z_buf, 0, 0, nseg)
            tf_ap, tf_buf = pv_merge(ctx, tag, h, g, pvb, pvbufs, nseg, 0,
                                     g % 2)
            o_buf = ctx.OUTS.buf(("o", g % 4))
            o_ap = ctx.OUTS.ap[:, (g % 4) * GW:(g % 4 + 1) * GW]
            S.op("vector", lambda e, o=o_ap, a=tf_ap, b=rzb_ap:
                 e.tensor_mul(o, a, b),
                 reads=[tf_buf, rzb_buf], writes=[o_buf], tag=f"{tag}n{h}_{g}")
            spill_store(ctx, tag, h * 128, g, o_ap, o_buf)

    emit_wo_residual(ctx, src_ap, src_buf, dst_ap, dst_buf,
                     ctx.dram["w_l1b_oT"], None, tag)



def emit_sublayer(ctx, sub, src, dst):
    src_ap, src_buf = src
    dst_ap, dst_buf = dst
    if sub == "l0a":
        emit_l0a(ctx, src_ap, src_buf, dst_ap, dst_buf)
    elif sub.startswith("ffn"):
        emit_ffn(ctx, src_ap, src_buf, dst_ap, dst_buf, sub)
    elif sub == "l0b":
        emit_l0b(ctx, src_ap, src_buf, dst_ap, dst_buf)
    elif sub == "l1a":
        emit_l1a(ctx, src_ap, src_buf, dst_ap, dst_buf)
    elif sub == "l1b":
        emit_l1b(ctx, src_ap, src_buf, dst_ap, dst_buf)
    else:
        raise ValueError(sub)


def build_program(subs, host_arrays):
    nc = bass.Bass("TRN2", target_bir_lowering=False, debug=False)
    ctx = Ctx()
    ctx.nc = nc
    S = Sched()
    ctx.S = S
    need = set()
    for s in subs:
        need.update(sub_weight_names(s))
    ctx.dram = {}
    input_names = ["xin"]
    ctx.dram["xin"] = nc.dram_tensor("xin", [D, T], F32, kind="ExternalInput").ap()
    for nm in sorted(need):
        arr = host_arrays[nm]
        dt = BF16 if arr.dtype == BF else F32
        ctx.dram[nm] = nc.dram_tensor(nm, list(arr.shape), dt,
                                      kind="ExternalInput").ap()
        input_names.append(nm)
    ctx.dram["xout"] = nc.dram_tensor("xout", [D, T], F32,
                                      kind="ExternalOutput").ap()
    ctx.dram["xA"] = nc.dram_tensor("xA", [D, T], F32, kind="Internal").ap()
    ctx.dram["xB"] = nc.dram_tensor("xB", [D, T], F32, kind="Internal").ap()
    ctx.dram["spill"] = nc.dram_tensor(
        "spill", [8192, T], BF16,
        kind="ExternalOutput" if DEBUG_SPILL else "Internal").ap()
    has_ffn1 = any(s.startswith("ffn1") for s in subs)
    if has_ffn1:
        ctx.dram["w2hat"] = nc.dram_tensor(
            "w2hat", [8192, D], BF16,
            kind="ExternalOutput" if DEBUG_W2HAT else "Internal").ap()
    ctx.spill_buf = Buf("spill")
    ctx.w2hat_buf = Buf("w2hat")
    dram_bufs = {nm: Buf(nm) for nm in ("xin", "xA", "xB", "xout")}

    with ExitStack() as st:
        sb = lambda name, shape, dt: st.enter_context(nc.sbuf_tensor(name, shape, dt))
        ctx.LNXT = Region(sb("LNXT", [128, EC * T], BF16), "LNXT")
        ctx.XQ = Region(sb("XQ", [128, 8192], BF16), "XQ")
        ctx.XR = Region(sb("XR", [128, 5 * GW], F32), "XR")
        ctx.WA = Region(sb("WA", [128, 8192], BF16), "WA")
        ctx.WB = Region(sb("WB", [128, 8192], BF16), "WB")
        ctx.WC = Region(sb("WC", [128, 8192], BF16), "WC")
        ctx.QK = Region(sb("QK", [128, 8192], BF16), "QK")
        ctx.PRB = Region(sb("PRB", [128, 8 * GW], BF16), "PRB")
        ctx.TMPF = Region(sb("TMPF", [128, 4 * GW], F32), "TMPF")
        ctx.TAB = Region(sb("TAB", [128, 8192], BF16), "TAB")
        ctx.SMF = Region(sb("SMF", [128, 512], F32), "SMF")
        ctx.OUTS = Region(sb("OUTS", [128, 4 * GW], BF16), "OUTS")
        ctx.MISC = Region(sb("MISC", [128, 1024], F32), "MISC")
        ctx.MISB = Region(sb("MISB", [128, 1024], BF16), "MISB")
        ctx.RZ = Region(sb("RZ", [128, 512], F32), "RZ")
        ctx.PP = Region(sb("PP", [128, 3072], BF16), "PP")
        ctx.PS = [Region(st.enter_context(nc.psum_tensor(f"ps{i}", [128, 512],
                                                         F32)),
                         f"ps{i}", single=True) for i in range(7)]
        ctx.PST = Region(st.enter_context(nc.psum_tensor("pst", [128, 1024],
                                                         BF16)),
                         "pst", single=True)

        sems = {}
        for e in ENGINES:
            sems[("cmp", e)] = st.enter_context(nc.semaphore(f"c_{e}"))
            sems[("dma", e)] = st.enter_context(nc.semaphore(f"d_{e}"))
            sems[("dmat", e)] = st.enter_context(nc.semaphore(f"t_{e}"))

        if has_ffn1:
            emit_w2hat(ctx)
        cur = ("xin", dram_bufs["xin"])
        pool = ["xA", "xB"]
        for i, s in enumerate(subs):
            dname = "xout" if i == len(subs) - 1 else pool[i % 2]
            emit_sublayer(ctx, s, (ctx.dram[cur[0]], cur[1]),
                          (ctx.dram[dname], dram_bufs[dname]))
            cur = (dname, dram_bufs[dname])

        blk = st.enter_context(nc.Block())
        S.finalize_and_emit(blk, sems)
    return nc, input_names


def kernel(**inputs):
    consts = _host_constants()
    w = _prep_weights(inputs)
    host_arrays = {**consts, **w}
    x = np.asarray(inputs["L0_attn_a_x"])
    xTs = [_cc(x[b].T) for b in range(NCORE)]

    groups = [SUBLAYERS] if MODE == "fused" else [[s] for s in SUBLAYERS]
    cur = xTs
    for subs in groups:
        nc, input_names = build_program(subs, host_arrays)
        in_maps = []
        for b in range(NCORE):
            m = {"xin": cur[b]}
            for nm in input_names:
                if nm != "xin":
                    m[nm] = host_arrays[nm]
            in_maps.append(m)
        res = run_bass_kernel_spmd(nc, in_maps, core_ids=list(range(NCORE)))
        cur = [np.asarray(r["xout"]) for r in res.results]

    return np.stack([c.T for c in cur], axis=0).astype(np.float32)
